# version 06
# baseline (speedup 1.0000x reference)
import numpy as np
import concourse.bass as bass
import concourse.mybir as mybir
from concourse.bass_utils import run_bass_kernel_spmd
from concourse.bass_core_rust import IndirectOffsetOnAxis

F32 = mybir.dt.float32
U32 = mybir.dt.uint32
I32 = mybir.dt.int32
AF = mybir.ActivationFunctionType
ALU = mybir.AluOpType
AX = mybir.AxisListType
EngT = mybir.EngineType

N_CORES = 8
B_, S_, D_, H_ = 4, 2048, 2048, 16
DH = 128
NTOK = 8192
TPC = 1024
NCH = 16
INTER = 8192
TOPK = 8
EPS = 1e-5
BIGNEG = -1.0e30
PI = float(np.pi)

DEBUG = False
STOP_KV = False
F_LNF = False
F_SIL = False
F_H = False
F_O = False
ENGINES = ("sync", "gpsimd", "tensor", "vector", "scalar")
OWN_SEM = {"vector": "v", "scalar": "s", "tensor": "t"}


class Emitter:

    def __init__(self, nc):
        self.nc = nc
        self.streams = {e: [] for e in ENGINES}
        self.sems = {}
        self.cnt = {}

    def sem(self, name):
        if name not in self.sems:
            self.sems[name] = self.nc.alloc_semaphore(name)
            self.cnt[name] = 0
        return self.sems[name]

    def op(self, eng, make, waits=(), inc=None, dma=False):
        own = OWN_SEM.get(eng)
        ws = []
        for w in waits:
            if w is None:
                continue
            s, v = w
            if s == own:
                continue
            assert v <= self.cnt.get(s, 0), f"future token {s}={v}>{self.cnt.get(s, 0)}"
            ws.append((s, v))
        if eng == "vector" and inc is None:
            inc = "v"
        if inc is not None:
            self.sem(inc)
            self.cnt[inc] += 16 if dma else 1
        self.streams[eng].append((make, tuple(ws), inc, dma))
        if eng == "vector":
            self.streams[eng].append((None, ((inc, self.cnt[inc]),), None, False))

    def vop(self, make, waits=(), inc="v"):
        self.op("vector", make, waits=waits, inc=inc)

    def tok(self, name):
        return (name, self.cnt.get(name, 0))

    def emit(self):
        with self.nc.Block() as block:
            for e in ENGINES:
                stream = self.streams[e]

                def body(engine, stream=stream):
                    last = {}
                    for make, waits, inc, dma in stream:
                        if make is None:
                            for s, v in waits:
                                engine.wait_ge(self.sems[s], v)
                                last[s] = max(last.get(s, -1), v)
                            continue
                        for s, v in waits:
                            if last.get(s, -1) < v:
                                engine.wait_ge(self.sems[s], v)
                                last[s] = v
                        ins = make(engine)
                        if inc is not None:
                            ins.then_inc(self.sems[inc], 16 if dma else 1)
                getattr(block, e)(body)


def build_megakernel():
    nc = bass.Bass("TRN2", target_bir_lowering=False, num_devices=N_CORES,
                   detect_race_conditions=False)
    em = Emitter(nc)
    o, t = em.op, em.tok

    def din(name, shape, dtype=F32):
        return nc.dram_tensor(name, list(shape), dtype, kind="ExternalInput").ap()

    xT = din("xT", [D_, TPC])
    WqT = din("WqT", [D_, D_]); WkT = din("WkT", [D_, D_])
    WvT = din("WvT", [D_, D_]); WoT = din("WoT", [D_, D_])
    bq = din("bq", [D_]); bk = din("bk", [D_]); bv = din("bv", [D_]); bo = din("bo", [D_])
    spiral = din("spiral", [H_]); helix = din("helix", [D_])
    W_q = din("W_q", [D_, 128]); keys_mine = din("keys_mine", [64, 64])
    down_embed = din("down_embed", [4096, D_]); up_embed = din("up_embed", [4096, D_])
    gate_w = din("gate_w", [D_, INTER]); up_w = din("up_w", [D_, INTER])
    down_w = din("down_w", [INTER, D_])
    agoff = din("agoff", [1, 2], I32)
    slot_in = din("slot", [1, 1], I32)
    yT = nc.dram_tensor("yT", [D_, TPC], F32, kind="ExternalOutput").ap()
    ident_d = nc.inline_tensor(np.eye(128, dtype=np.float32), "ident")

    K_sh = nc.dram_tensor("K_sh", [2 * D_, TPC], U32, addr_space="Shared").ap()
    V_sh = nc.dram_tensor("V_sh", [2 * TPC, D_], U32, addr_space="Shared").ap()
    Q_loc = nc.dram_tensor("Q_loc", [D_, TPC], F32,
                           kind="ExternalOutput" if DEBUG else "Internal").ap()
    K_loc = nc.dram_tensor("K_loc", [D_, TPC], F32).ap()
    V_loc = nc.dram_tensor("V_loc", [TPC, D_], F32).ap()
    X_stash = nc.dram_tensor("X_stash", [D_, TPC], F32).ap()
    LN_fm = nc.dram_tensor("LN_fm", [D_, TPC], F32).ap()
    LN_tm = nc.dram_tensor("LN_tm", [TPC, D_], F32).ap()
    if DEBUG:
        DBG_X1 = nc.dram_tensor("DBG_X1", [D_, TPC], F32, kind="ExternalOutput").ap()
        DBG_X2 = nc.dram_tensor("DBG_X2", [D_, TPC], F32, kind="ExternalOutput").ap()
        DBG_LNTM = nc.dram_tensor("DBG_LNTM", [TPC, D_], F32, kind="ExternalOutput").ap()
        DBG_S8 = nc.dram_tensor("DBG_S8", [128, 16, 8], F32, kind="ExternalOutput").ap()
        DBG_I8 = nc.dram_tensor("DBG_I8", [128, 16, 8], F32, kind="ExternalOutput").ap()
        DBG_SX = nc.dram_tensor("DBG_SX", [128, 8, 16], F32, kind="ExternalOutput").ap()
        DBG_SY = nc.dram_tensor("DBG_SY", [128, 8, 16], F32, kind="ExternalOutput").ap()
        DBG_QT = nc.dram_tensor("DBG_QT", [128, 16, 64], F32, kind="ExternalOutput").ap()
        DBG_QLO = nc.dram_tensor("DBG_QLO", [64, 1024], F32, kind="ExternalOutput").ap()
        DBG_MLP = nc.dram_tensor("DBG_MLP", [D_, TPC], F32, kind="ExternalOutput").ap()
        DBG_H = nc.dram_tensor("DBG_H", [128, 8, 512], F32, kind="ExternalOutput").ap()
        DBG_LNF = nc.dram_tensor("DBG_LNF", [128, 16, 512], F32, kind="ExternalOutput").ap()
        DBG_SIL = nc.dram_tensor("DBG_SIL", [128, 512], F32, kind="ExternalOutput").ap()
        DBG_UP = nc.dram_tensor("DBG_UP", [128, 512], F32, kind="ExternalOutput").ap()
        DBG_LNFM_RAW = nc.dram_tensor("DBG_LNFM_RAW", [D_, TPC], F32, kind="ExternalOutput").ap()
        DBG_O = nc.dram_tensor("DBG_O", [D_, TPC], F32, kind="ExternalOutput").ap()
        DBG_KSH = nc.dram_tensor("DBG_KSH", [2 * D_, TPC], F32, kind="ExternalOutput").ap()
        DBG_KLOC = nc.dram_tensor("DBG_KLOC", [D_, TPC], F32, kind="ExternalOutput").ap()
        DBG_CONST = nc.dram_tensor("DBG_CONST", [128, 448], F32, kind="ExternalOutput").ap()
        DBG_TA0 = nc.dram_tensor("DBG_TA0", [128, 2048], F32, kind="ExternalOutput").ap()
        DBG_LN2 = nc.dram_tensor("DBG_LN2", [D_, TPC], F32, kind="ExternalOutput").ap()
        DBG_STAT = nc.dram_tensor("DBG_STAT", [128, 24], F32, kind="ExternalOutput").ap()
    bar_in = nc.dram_tensor("bar_in", [1, 4], F32).ap()
    bar_out = nc.dram_tensor("bar_out", [1, 4], F32).ap()
    ag_in = [nc.dram_tensor(f"ag_in{i}", [2 * TPC, 16], F32).ap() for i in range(2)]
    ag_out = [nc.dram_tensor(f"ag_out{i}", [N_CORES * 2 * TPC, 16], F32).ap()
              for i in range(2)]

    sb = nc.alloc_sbuf_tensor
    XT0 = sb("XT0", [128, NCH, TPC], F32)
    AR = sb("AR", [128, NCH, TPC], F32)
    WB = sb("WB", [128, 8, TPC], F32)
    TA = sb("TA", [128, 2048], F32)
    TB = sb("TB", [128, 2048], F32)
    SIL = sb("SIL", [128, 512], F32)
    SM = sb("SM", [128, 1024], F32)
    SM2 = sb("SM2", [128, 1024], F32)
    SM3 = sb("SM3", [128, 1024], F32)
    QLO = sb("QLO", [64, 1024], F32)
    QHI = sb("QHI", [64, 1024], F32)
    QT = sb("QT", [128, 16, 64], F32)
    S8 = sb("S8", [128, 16, 8], F32)
    I8 = sb("I8", [128, 16, 8], F32)
    SX8 = sb("SX8", [128, 8, 16], F32)
    SY8 = sb("SY8", [128, 8, 16], F32)
    CMB = sb("CMB", [128, 64], F32)
    CMI = sb("CMI", [128, 64], F32)
    EW = sb("EW", [128, 8], F32)
    EIDI = sb("EIDI", [128, 8], I32)
    STAT = sb("STAT", [128, 24], F32)
    CONST = sb("CONST", [128, 448], F32)
    IDT = CONST[:, 0:128]
    ONES = CONST[:, 128:129]
    QSC = CONST[:, 130:146]
    KSC = CONST[:, 146:162]
    BQc = CONST[:, 162:178]
    BKc = CONST[:, 178:194]
    BOc = CONST[:, 194:210]
    QBE = CONST[:, 210:226]
    KBE = CONST[:, 226:242]
    IOTA = CONST[:, 242:306]
    ONESR = CONST[0:1, 312:440]
    BARZ = CONST[0:1, 440:444]
    SL = sb("SL", [1, 8], I32)

    ps_big = nc.alloc_psum_tensor("ps_big", [128, 2048], F32)
    ps_c = nc.alloc_psum_tensor("ps_c", [128, 1024], F32)
    ps_a = nc.alloc_psum_tensor("ps_a", [128, 512], F32)
    ps_b = nc.alloc_psum_tensor("ps_b", [128, 512], F32)
    assert nc.sbuf_bytes_remaining >= 0, nc.sbuf_bytes_remaining

    pfree = {"big": None, "c": None, "a": None, "b": None}

    o("sync", lambda e: e.dma_start(out=XT0.ap(),
                                    in_=xT.rearrange("(c p) t -> p c t", p=128)),
      inc="w", dma=True)
    ldx = t("w")
    o("sync", lambda e: e.dma_start(out=IDT, in_=ident_d.ap()), inc="w", dma=True)
    o("sync", lambda e: e.dma_start(out=TA[0:16, 0:128],
                                    in_=helix.rearrange("(h d) -> h d", h=16)),
      inc="w", dma=True)
    o("sync", lambda e: e.dma_start(out=TA[0:16, 128:129],
                                    in_=spiral.unsqueeze(1)), inc="w", dma=True)
    for i, src in enumerate((bq, bk, bo)):
        o("sync", lambda e, s=src, i=i: e.dma_start(
            out=TA[0:16, 512 + 128 * i:640 + 128 * i],
            in_=s.rearrange("(c d) -> c d", c=16)), inc="w", dma=True)
    o("sync", lambda e: e.dma_start(out=SL[0:1, 0:1], in_=slot_in), inc="w", dma=True)
    o("sync", lambda e: e.dma_start(out=SL[0:1, 2:4], in_=agoff), inc="w", dma=True)
    o("sync", lambda e: e.dma_start(out=TB[0:1, :], in_=bv.unsqueeze(0)),
      inc="w", dma=True)
    ldc = t("w")

    o("vector", lambda e: e.memset(ONES, 1.0), waits=[ldc], inc="v")
    o("vector", lambda e: e.memset(ONESR, 1.0), inc="v")
    o("vector", lambda e: e.memset(STAT[:, 20:21], float(EPS)), inc="v")
    o("vector", lambda e: e.memset(BARZ, 0.0), inc="v")
    o("gpsimd", lambda e: e.iota(IOTA, [[1, 64]], channel_multiplier=0,
                                 allow_small_or_imprecise_dtypes=True),
      waits=[ldc], inc="gc")
    iota_tok = t("gc")

    def wrap_sin(dst, shift):
        ga, gb = SM[0:16, 0:128], SM[0:16, 128:256]
        o("vector", lambda e: e.tensor_single_scalar(ga, TA[0:16, 0:128], PI - shift,
                                                     ALU.is_gt), waits=[ldc], inc="v")
        o("vector", lambda e: e.tensor_single_scalar(gb, TA[0:16, 0:128], -PI - shift,
                                                     ALU.is_lt), inc="v")
        o("vector", lambda e: e.tensor_single_scalar(dst, TA[0:16, 0:128], shift,
                                                     ALU.add), inc="v")
        o("vector", lambda e: e.scalar_tensor_tensor(dst, ga, -2 * PI, dst,
                                                     ALU.mult, ALU.add), inc="v")
        o("vector", lambda e: e.scalar_tensor_tensor(dst, gb, 2 * PI, dst,
                                                     ALU.mult, ALU.add), inc="v")
        wv = t("v")
        o("scalar", lambda e: e.activation(dst, dst, AF.Sin), waits=[wv], inc="s")
        return t("s")

    karg = TA[0:16, 256:384]
    qarg = TA[0:16, 384:512]
    ks_tok = wrap_sin(karg, 0.0)
    qs_tok = wrap_sin(qarg, PI / 2)
    o("vector", lambda e: e.tensor_scalar_mul(TA[0:16, 128:129], TA[0:16, 128:129],
                                              1.0 / float(np.sqrt(DH))),
      waits=[ldc], inc="v")
    o("vector", lambda e: e.tensor_single_scalar(qarg, qarg, TA[0:16, 128:129],
                                                 ALU.mult), waits=[qs_tok], inc="v")
    sc_rows = t("v")
    o("tensor", lambda e: e.transpose(ps_a[:, 0:16], karg, IDT[0:16, 0:16]),
      waits=[sc_rows, ks_tok, ldc], inc="t")
    o("tensor", lambda e: e.transpose(ps_a[:, 16:32], qarg, IDT[0:16, 0:16]), inc="t")
    for i in range(3):
        o("tensor", lambda e, i=i: e.transpose(
            ps_a[:, 32 + 16 * i:48 + 16 * i], TA[0:16, 512 + 128 * i:640 + 128 * i],
            IDT[0:16, 0:16]), inc="t")
    trs = t("t")
    o("vector", lambda e: e.tensor_copy(KSC, ps_a[:, 0:16]), waits=[trs], inc="v")
    o("vector", lambda e: e.tensor_copy(QSC, ps_a[:, 16:32]), inc="v")
    o("vector", lambda e: e.tensor_copy(BQc, ps_a[:, 32:48]), inc="v")
    o("vector", lambda e: e.tensor_copy(BKc, ps_a[:, 48:64]), inc="v")
    o("vector", lambda e: e.tensor_copy(BOc, ps_a[:, 64:80]), inc="v")
    o("vector", lambda e: e.tensor_mul(QBE, QSC, BQc), inc="v")
    o("vector", lambda e: e.tensor_mul(KBE, KSC, BKc), inc="v")
    scales_tok = t("v")
    pfree["a"] = scales_tok

    if DEBUG:
        o("gpsimd", lambda e: e.dma_start(out=DBG_CONST, in_=CONST[:, :]),
          waits=[scales_tok], inc="g", dma=True)
    BVB = WB.ap()[:, 6:8, :].rearrange("p a f -> p (a f)")
    for q in range(4):
        o("tensor", lambda e, q=q: e.matmul(
            ps_big[:, q * 512:(q + 1) * 512], ONESR,
            TB[0:1, q * 512:(q + 1) * 512], start=True, stop=True),
            waits=[ldc, t("v")] if q == 0 else (), inc="t" if q == 3 else None)
    bvm = t("t")
    o("vector", lambda e: e.tensor_copy(BVB, ps_big[:]), waits=[bvm], inc="v")
    bvb_tok = t("v")
    pfree["big"] = bvb_tok

    regv = {}

    def mkreg(e, name, apsl):
        r = e.alloc_register(name)
        regv[name + "_reg"] = r
        return e.reg_load(r, apsl)

    o("gpsimd", lambda e: mkreg(e, "slot", SL[0:1, 0:1]), waits=[ldc])
    o("gpsimd", lambda e: mkreg(e, "offsx", SL[0:1, 2:3]), waits=[ldc])
    o("gpsimd", lambda e: mkreg(e, "offsy", SL[0:1, 3:4]), waits=[ldc])

    def snapv(name, minv, maxv):
        key = name + "_snap"
        if key not in regv:
            regv[key] = nc.snap(regv[name + "_reg"], engines=[EngT.Pool],
                                min_val=minv, max_val=maxv)
        return regv[key]

    def ln_tile(XB, tile, dst_fm, xtok, ta_waits=(), tb_waits=()):
        tb = tile * 128
        for c in range(NCH):
            o("tensor", lambda e, c=c: e.transpose(
                ps_big[:, c * 128:(c + 1) * 128], XB[:, c, tb:tb + 128], IDT),
                waits=[xtok, ldc, pfree["big"]] if c == 0 else (),
                inc="t" if c == NCH - 1 else None)
        tr1 = t("t")
        o("vector", lambda e: e.tensor_copy(TA[:], ps_big[:]),
          waits=[tr1] + list(ta_waits), inc="v")
        xtm = t("v")
        pfree["big"] = xtm
        em.vop(lambda e: e.reduce_sum(
            STAT[:, 16:17].unsqueeze(1),
            TA[:].unsqueeze(1), AX.X), inc="v")
        o("scalar", lambda e: e.activation(TB[:], TA[:], AF.Square),
          waits=[xtm] + list(tb_waits), inc="s")
        sq = t("s")
        em.vop(lambda e: e.reduce_sum(
            STAT[:, 17:18].unsqueeze(1),
            TB[:].unsqueeze(1), AX.X), waits=[sq], inc="v")
        o("vector", lambda e: e.tensor_scalar_mul(STAT[:, 16:17], STAT[:, 16:17],
                                                  1.0 / D_), inc="v")
        o("vector", lambda e: e.tensor_scalar_mul(STAT[:, 17:18], STAT[:, 17:18],
                                                  1.0 / D_), inc="v")
        o("vector", lambda e: e.tensor_mul(STAT[:, 18:19], STAT[:, 16:17],
                                           STAT[:, 16:17]), inc="v")
        o("vector", lambda e: e.tensor_sub(STAT[:, 17:18], STAT[:, 17:18],
                                           STAT[:, 18:19]), inc="v")
        o("vector", lambda e: e.tensor_scalar_add(STAT[:, 17:18], STAT[:, 17:18],
                                                  float(EPS)), inc="v")
        vv = t("v")
        o("scalar", lambda e: e.activation(STAT[:, 19:20], STAT[:, 17:18], AF.Sqrt),
          waits=[vv], inc="s")
        sq2 = t("s")
        rcol = STAT[:, 2 * tile:2 * tile + 1]
        bcol = STAT[:, 2 * tile + 1:2 * tile + 2]
        em.vop(lambda e: e.reciprocal(rcol, STAT[:, 19:20]), waits=[sq2], inc="v")
        for _newton in range(2):
            o("vector", lambda e: e.tensor_mul(STAT[:, 21:22], rcol, rcol), inc="v")
            o("vector", lambda e: e.tensor_mul(STAT[:, 21:22], STAT[:, 21:22],
                                               STAT[:, 17:18]), inc="v")
            o("vector", lambda e: e.tensor_scalar(STAT[:, 21:22], STAT[:, 21:22],
                                                  3.0, -0.5, ALU.subtract, ALU.mult),
              inc="v")
            o("vector", lambda e: e.tensor_mul(rcol, rcol, STAT[:, 21:22]), inc="v")
        o("vector", lambda e: e.scalar_tensor_tensor(bcol, STAT[:, 16:17], -1.0, rcol,
                                                     ALU.mult, ALU.mult), inc="v")
        stats = t("v")
        o("vector", lambda e: e.tensor_scalar(TB[:], TA[:], rcol, bcol,
                                              ALU.mult, ALU.add),
          waits=[stats], inc="v")
        lntm = t("v")
        for c in range(NCH):
            o("tensor", lambda e, c=c: e.transpose(
                ps_big[:, c * 128:(c + 1) * 128], TB[:, c * 128:(c + 1) * 128], IDT),
                waits=[lntm, pfree["big"]] if c == 0 else (),
                inc="t" if c == NCH - 1 else None)
        tr2 = t("t")
        dst = dst_fm if dst_fm is not None else TA[:].rearrange("p (c t) -> p c t", c=NCH)
        o("vector", lambda e: e.tensor_copy(
            dst, ps_big[:].rearrange("p (c t) -> p c t", c=NCH)), waits=[tr2], inc="v")
        fm = t("v")
        pfree["big"] = fm
        return lntm, fm

    def ffn(call, XB, SC, xtok):
        SCv = SC.ap().rearrange("p c (a f) -> p (c a) f", a=2)
        HSL = [SCv[:, 16 + i, :] for i in range(8)]
        GB1 = SC.ap()[:, 12:14, :].rearrange("p a f -> p (a f)")
        GB2 = SC.ap()[:, 14:16, :].rearrange("p a f -> p (a f)")
        agi, ago = ag_in[call], ag_out[call]

        WGU = [WB.ap()[:, 0:2, :].rearrange("p a (c f) -> p (a c) f", f=128),
               WB.ap()[:, 2:4, :].rearrange("p a (c f) -> p (a c) f", f=128)]
        WDN = [WB.ap()[:, 4:5, :].rearrange("p a (c f) -> p (a c) f", f=128),
               WB.ap()[:, 5:6, :].rearrange("p a (c f) -> p (a c) f", f=128)]
        gu_free = [xtok, xtok]
        dn_free = [xtok, xtok]

        o("sync", lambda e: e.dma_start(
            out=WGU[0][:, 0:16, :], in_=W_q.rearrange("(c p) f -> p c f", p=128)),
            waits=[gu_free[0]], inc="w", dma=True)
        wq_l = t("w")
        guard_ta = []
        guard_tb = []
        for tile in range(8):
            lntm, fm = ln_tile(XB, tile, None, xtok,
                               ta_waits=guard_ta, tb_waits=guard_tb)
            o("gpsimd", lambda e, tile=tile: e.dma_start(
                out=LN_tm[tile * 128:(tile + 1) * 128, :], in_=TB[:]),
                waits=[lntm], inc="g", dma=True)
            tm_st = t("g")
            o("gpsimd", lambda e, tile=tile: e.dma_start(
                out=LN_fm.rearrange("(c p) t -> p c t", p=128)[:, :,
                                                              tile * 128:(tile + 1) * 128],
                in_=TA[:].rearrange("p (c t) -> p c t", c=NCH)),
                waits=[fm], inc="g", dma=True)
            fm_st = t("g")
            for g, (qdst, psx, pn) in enumerate(((QLO, ps_a, "a"), (QHI, ps_b, "b"))):
                for c in range(NCH):
                    o("tensor", lambda e, c=c, g=g, psx=psx: e.matmul(
                        psx[0:64, 0:128], WGU[0][:, c, g * 64:(g + 1) * 64],
                        TA[:, c * 128:(c + 1) * 128],
                        start=(c == 0), stop=(c == NCH - 1)),
                        waits=[wq_l, fm, pfree[pn]] if c == 0 else (),
                        inc="t" if c == NCH - 1 else None)
                qmm = t("t")
                o("vector", lambda e, tile=tile, qdst=qdst, psx=psx: e.tensor_copy(
                    qdst[:, tile * 128:(tile + 1) * 128], psx[0:64, 0:128]),
                    waits=[qmm], inc="v")
                pfree[pn] = t("v")
            guard_ta = [fm_st, qmm]
            guard_tb = [tm_st]
        ln_all = t("g")
        gu_free[0] = t("t")
        if DEBUG and call == 0:
            o("gpsimd", lambda e: e.dma_start(out=DBG_LNTM, in_=LN_tm),
              waits=[ln_all], inc="g", dma=True)
            o("gpsimd", lambda e: e.dma_start(out=DBG_LNFM_RAW, in_=LN_fm),
              waits=[ln_all], inc="g", dma=True)

        o("sync", lambda e: e.dma_start(out=WGU[1][0:64, 0, 0:64], in_=keys_mine),
          waits=[gu_free[1]], inc="w", dma=True)
        keysl = t("w")
        qfin = t("v")
        for half in range(2):
            o("tensor", lambda e, half=half: e.matmul(
                ps_c[0:64, half * 512:(half + 1) * 512], WGU[1][0:64, 0, 0:64],
                QLO[:, half * 512:(half + 1) * 512], start=True, stop=True),
                waits=[keysl, qfin, pfree["c"]] if half == 0 else (),
                inc="t" if half == 1 else None)
        rmm0 = t("t")
        for half in range(2):
            o("tensor", lambda e, half=half: e.matmul(
                ps_big[0:64, half * 512:(half + 1) * 512], WGU[1][0:64, 0, 0:64],
                QHI[:, half * 512:(half + 1) * 512], start=True, stop=True),
                waits=[pfree["big"]] if half == 0 else (),
                inc="t" if half == 1 else None)
        rmm = t("t")
        gu_free[1] = rmm
        o("vector", lambda e: e.tensor_copy(SM3[0:64, :], ps_c[0:64, :]),
          waits=[rmm0], inc="v")
        r0ev = t("v")
        pfree["c"] = r0ev
        o("vector", lambda e: e.tensor_copy(SM[0:64, :], ps_big[0:64, 0:1024]),
          waits=[rmm], inc="v")
        r1ev = t("v")
        pfree["big"] = r1ev
        for p in range(2):
            srcbuf = SM3 if p == 0 else SM
            for jb in range(8):
                pn = "a" if (p * 8 + jb) % 2 == 0 else "b"
                psx = ps_a if (p * 8 + jb) % 2 == 0 else ps_b
                o("tensor", lambda e, p=p, jb=jb, psx=psx, srcbuf=srcbuf: e.transpose(
                    psx[:, 0:64], srcbuf[0:64, jb * 128:(jb + 1) * 128],
                    IDT[0:64, 0:64]),
                    waits=[r0ev if p == 0 else r1ev, pfree[pn]], inc="t")
                trq = t("t")
                o("vector", lambda e, p=p, jb=jb, psx=psx: e.tensor_copy(
                    QT[:, p * 8 + jb, :], psx[:, 0:64]), waits=[trq], inc="v")
                pfree[pn] = t("v")
        qt_tok = t("v")
        if DEBUG and call == 0:
            o("gpsimd", lambda e: e.dma_start(out=DBG_QT, in_=QT.ap()),
              waits=[qt_tok], inc="g", dma=True)
            o("gpsimd", lambda e: e.dma_start(out=DBG_QLO, in_=QLO.ap()),
              waits=[qt_tok], inc="g", dma=True)
        qt_dump = t("g")

        m3 = SM[:].rearrange("p (n k) -> p n k", n=16)
        m4 = SM2[:].rearrange("p (n k) -> p n k", n=16)
        first = True
        for k in range(TOPK):
            em.vop(lambda e, k=k: e.reduce_max(S8[:, :, k:k + 1], QT[:], AX.X),
              waits=[qt_tok, iota_tok] if first else (), inc="v")
            first = False
            o("vector", lambda e, k=k: e.tensor_tensor(
                m3, QT[:], S8[:, :, k:k + 1].to_broadcast([128, 16, 64]),
                ALU.is_equal), inc="v")
            o("vector", lambda e: e.tensor_tensor(
                m4, m3, IOTA.unsqueeze(1).to_broadcast([128, 16, 64]),
                ALU.mult), inc="v")
            o("vector", lambda e: e.scalar_tensor_tensor(
                m4, m3, 64.0, m4, ALU.mult, ALU.subtract), inc="v")
            em.vop(lambda e, k=k: e.reduce_max(I8[:, :, k:k + 1], m4, AX.X),
              inc="v")
            o("vector", lambda e, k=k: e.tensor_scalar(
                I8[:, :, k:k + 1], I8[:, :, k:k + 1], 64.0, -1.0,
                ALU.subtract, ALU.mult), inc="v")
            o("vector", lambda e, k=k: e.tensor_tensor(
                m3, IOTA.unsqueeze(1).to_broadcast([128, 16, 64]),
                I8[:, :, k:k + 1].to_broadcast([128, 16, 64]), ALU.is_equal), inc="v")
            o("vector", lambda e: e.scalar_tensor_tensor(
                QT[:], m3, BIGNEG, QT[:], ALU.mult, ALU.add), inc="v")
        tk_tok = t("v")
        if DEBUG and call == 0:
            o("gpsimd", lambda e: e.dma_start(out=DBG_S8, in_=S8.ap()),
              waits=[tk_tok], inc="g", dma=True)
            o("gpsimd", lambda e: e.dma_start(out=DBG_I8, in_=I8.ap()),
              waits=[tk_tok], inc="g", dma=True)

        for p in range(2):
            o("gpsimd", lambda e, p=p: e.dma_start(
                out=bass.AP(agi.tensor, p * 16, [[32, 128], [4096, 8], [1, 8]]),
                in_=S8[:, p * 8:(p + 1) * 8, :]), waits=[tk_tok], inc="g", dma=True)
            o("gpsimd", lambda e, p=p: e.dma_start(
                out=bass.AP(agi.tensor, p * 16 + 8, [[32, 128], [4096, 8], [1, 8]]),
                in_=I8[:, p * 8:(p + 1) * 8, :]), inc="g", dma=True)
        agw = t("g")
        o("gpsimd", lambda e: e.collective_compute(
            "AllGather", ALU.bypass, replica_groups=[list(range(N_CORES))],
            ins=[agi], outs=[ago]), waits=[agw], inc="gc")
        cc_tok = t("gc")
        o("gpsimd", lambda e: e.dma_start(
            out=SX8.ap(),
            in_=ago[bass.ds(snapv("offsx", 0, N_CORES * 2 * TPC - TPC), TPC), :]
            .rearrange("(n p) k -> p n k", p=128)), waits=[cc_tok], inc="g", dma=True)
        o("gpsimd", lambda e: e.dma_start(
            out=SY8.ap(),
            in_=ago[bass.ds(snapv("offsy", 0, N_CORES * 2 * TPC - TPC), TPC), :]
            .rearrange("(n p) k -> p n k", p=128)), inc="g", dma=True)
        sxy = t("g")
        if DEBUG and call == 0:
            o("gpsimd", lambda e: e.dma_start(out=DBG_SX, in_=SX8.ap()),
              waits=[sxy], inc="g", dma=True)
            o("gpsimd", lambda e: e.dma_start(out=DBG_SY, in_=SY8.ap()),
              waits=[sxy], inc="g", dma=True)

        for Hh in range(2):
            hb = Hh * 512
            o("sync", lambda e, hb=hb: e.dma_start(
                out=SCv[:, 0:16, :],
                in_=LN_fm.rearrange("(c p) t -> p c t", p=128)[:, :, hb:hb + 512]),
                waits=[ln_all, t("t")], inc="w", dma=True)
            lnh = t("w")
            if DEBUG and call == 0 and Hh == 0:
                o("gpsimd", lambda e: e.dma_start(out=DBG_LNF, in_=SCv[:, 0:16, :]),
                  waits=[lnh], inc="g", dma=True)
                lnh = t("g")
            elif F_LNF:
                o("gpsimd", lambda e: e.nop(), waits=[lnh], inc="gc")
                lnh = t("gc")
            for eighth in range(8):
                jlist = range(eighth * 8, eighth * 8 + 8)
                h_toks = []
                prev_h = pfree.get("hprev")
                for idx, j in enumerate(jlist):
                    buf = idx % 2
                    o("sync", lambda e, j=j, buf=buf: e.dma_start(
                        out=WGU[buf][:, 0:16, :],
                        in_=gate_w[:, j * 128:(j + 1) * 128].rearrange(
                            "(c p) f -> p c f", p=128)),
                        waits=[gu_free[buf]], inc="w", dma=True)
                    gl = t("w")
                    for c in range(NCH):
                        o("tensor", lambda e, c=c, buf=buf: e.matmul(
                            ps_a[:, 0:512], WGU[buf][:, c, :], SCv[:, c, :],
                            start=(c == 0), stop=(c == NCH - 1)),
                            waits=[gl, lnh, pfree["a"]] if c == 0 else (),
                            inc="t" if c == NCH - 1 else None)
                    gmm = t("t")
                    o("sync", lambda e, j=j, buf=buf: e.dma_start(
                        out=WGU[buf][:, 0:16, :],
                        in_=up_w[:, j * 128:(j + 1) * 128].rearrange(
                            "(c p) f -> p c f", p=128)),
                        waits=[gmm], inc="w", dma=True)
                    ul = t("w")
                    for c in range(NCH):
                        o("tensor", lambda e, c=c, buf=buf: e.matmul(
                            ps_b[:, 0:512], WGU[buf][:, c, :], SCv[:, c, :],
                            start=(c == 0), stop=(c == NCH - 1)),
                            waits=[ul, pfree["b"]] if c == 0 else (),
                            inc="t" if c == NCH - 1 else None)
                    umm = t("t")
                    gu_free[buf] = umm
                    o("scalar", lambda e: e.activation(SIL[:], ps_a[:, 0:512], AF.Silu),
                      waits=[gmm, prev_h], inc="s")
                    sil = t("s")
                    pfree["a"] = sil
                    if F_SIL:
                        o("gpsimd", lambda e: e.nop(), waits=[sil], inc="gc")
                        sil = t("gc")
                        pfree["a"] = sil
                    if DEBUG and call == 0 and Hh == 0 and eighth == 0 and idx == 0:
                        o("gpsimd", lambda e: e.dma_start(out=DBG_SIL, in_=SIL[:]),
                          waits=[sil], inc="g", dma=True)
                        o("vector", lambda e: e.tensor_copy(SM2[:, 0:512], ps_b[:, 0:512]),
                          waits=[t("t")], inc="v")
                        _upt = t("v")
                        o("gpsimd", lambda e: e.dma_start(out=DBG_UP, in_=SM2[:, 0:512]),
                          waits=[_upt], inc="g", dma=True)
                        sil = t("g")
                    o("vector", lambda e, idx=idx: e.tensor_tensor(
                        HSL[idx], SIL[:], ps_b[:, 0:512], ALU.mult),
                        waits=[sil, umm] + ([t("g")] if (DEBUG and call == 0 and
                                                         Hh == 0 and eighth == 1)
                                            else []) + ([dn_free[0], dn_free[1]]
                                                        if F_H else []), inc="v")
                    h_toks.append(t("v"))
                    pfree["b"] = h_toks[-1]
                    prev_h = h_toks[-1]
                    pfree["hprev"] = prev_h
                h_done = h_toks[-1]
                if DEBUG and call == 0 and Hh == 0 and eighth == 0:
                    for _i in range(8):
                        o("gpsimd", lambda e, _i=_i: e.dma_start(
                            out=DBG_H[:, _i, :], in_=HSL[_i]),
                            waits=[h_done], inc="g", dma=True)
                    _hdmp = t("g")
                for i in range(NCH):
                    buf = i % 2
                    o("sync", lambda e, i=i, eighth=eighth, buf=buf: e.dma_start(
                        out=WDN[buf][:, 0:8, :],
                        in_=down_w[eighth * 1024:(eighth + 1) * 1024,
                                   i * 128:(i + 1) * 128].rearrange(
                            "(c p) f -> p c f", p=128)),
                        waits=[dn_free[buf]], inc="w", dma=True)
                    dl = t("w")
                    pn = "a" if i % 2 == 0 else "b"
                    psx = ps_a if i % 2 == 0 else ps_b
                    for idx in range(8):
                        o("tensor", lambda e, idx=idx, buf=buf, psx=psx: e.matmul(
                            psx[:, 0:512], WDN[buf][:, idx, :], HSL[idx],
                            start=(idx == 0), stop=(idx == 7)),
                            waits=[dl, h_done, pfree[pn]] if idx == 0 else (),
                            inc="t" if idx == 7 else None)
                    dmm = t("t")
                    dn_free[buf] = dmm
                    o("vector", lambda e, i=i, hb=hb, psx=psx: e.scalar_tensor_tensor(
                        XB[:, i, hb:hb + 512], psx[:, 0:512], 0.5,
                        XB[:, i, hb:hb + 512], ALU.mult, ALU.add),
                        waits=[dmm], inc="v")
                    pfree[pn] = t("v")
        mlp_tok = t("v")
        if DEBUG and call == 0:
            o("gpsimd", lambda e: e.dma_start(
                out=DBG_MLP.rearrange("(c p) t -> p c t", p=128), in_=XB.ap()),
                waits=[mlp_tok], inc="g", dma=True)

        prev_mult = None
        gb2_free = None
        tb_free = None
        prev_silw = None
        for tile in range(8):
            o("gpsimd", lambda e, tile=tile: e.dma_start(
                out=TB[:], in_=LN_tm[tile * 128:(tile + 1) * 128, :]),
                waits=[ln_all, tb_free], inc="g", dma=True)
            lntm_t = t("g")
            sx = SX8[:, tile, 0:8]; ix = SX8[:, tile, 8:16]
            sy = SY8[:, tile, 0:8]; iy = SY8[:, tile, 8:16]
            o("vector", lambda e, ix=ix: e.tensor_scalar_mul(SM2[:, 32:40], ix, 64.0),
              waits=[sxy], inc="v")
            for a in range(8):
                o("vector", lambda e, a=a, sy=sy, sx=sx: e.tensor_single_scalar(
                    CMB[:, a * 8:(a + 1) * 8], sy, sx[:, a:a + 1], ALU.add), inc="v")
                o("vector", lambda e, a=a, iy=iy: e.tensor_single_scalar(
                    CMI[:, a * 8:(a + 1) * 8], iy, SM2[:, 32 + a:33 + a], ALU.add),
                    inc="v")
            built = t("v")
            cm3 = CMB[:].unsqueeze(1)
            for k in range(TOPK):
                em.vop(lambda e, k=k: e.reduce_max(
                    SM2[:, 0:1].unsqueeze(1), cm3, AX.X), inc="v")
                o("vector", lambda e, k=k: e.tensor_copy(EW[:, k:k + 1], SM2[:, 0:1]),
                  inc="v")
                o("vector", lambda e: e.tensor_single_scalar(
                    SM[:, 0:64], CMB[:], SM2[:, 0:1], ALU.is_equal), inc="v")
                o("vector", lambda e: e.tensor_mul(SM[:, 64:128], SM[:, 0:64], IOTA),
                  inc="v")
                o("vector", lambda e: e.scalar_tensor_tensor(
                    SM[:, 64:128], SM[:, 0:64], 64.0, SM[:, 64:128],
                    ALU.mult, ALU.subtract), inc="v")
                em.vop(lambda e: e.reduce_max(
                    SM2[:, 1:2].unsqueeze(1),
                    SM[:, 64:128].unsqueeze(1), AX.X), inc="v")
                o("vector", lambda e: e.tensor_scalar(
                    SM2[:, 1:2], SM2[:, 1:2], 64.0, -1.0, ALU.subtract, ALU.mult),
                    inc="v")
                o("vector", lambda e: e.tensor_single_scalar(
                    SM[:, 0:64], IOTA, SM2[:, 1:2], ALU.is_equal), inc="v")
                o("vector", lambda e: e.tensor_mul(SM[:, 64:128], SM[:, 0:64], CMI[:]),
                  inc="v")
                em.vop(lambda e, k=k: e.reduce_sum(
                    SM2[:, 2 + k:3 + k].unsqueeze(1),
                    SM[:, 64:128].unsqueeze(1), AX.X), inc="v")
                o("vector", lambda e: e.scalar_tensor_tensor(
                    CMB[:], SM[:, 0:64], BIGNEG, CMB[:], ALU.mult, ALU.add), inc="v")
            sel_tok = t("v")
            o("vector", lambda e: e.tensor_copy(EIDI[:], SM2[:, 2:10]),
              waits=[t("g")], inc="v")
            eid_tok = t("v")
            em.vop(lambda e: e.reduce_max(
                SM2[:, 10:11].unsqueeze(1),
                EW[:].unsqueeze(1), AX.X), inc="v")
            o("vector", lambda e: e.tensor_scalar_mul(SM2[:, 10:11], SM2[:, 10:11],
                                                      -1.0), inc="v")
            nm = t("v")
            o("scalar", lambda e: e.activation(EW[:], EW[:], AF.Exp, bias=SM2[:, 10:11],
                                               accum_out=SM2[:, 11:12]),
              waits=[nm], inc="s")
            ex = t("s")
            em.vop(lambda e: e.reciprocal(SM2[:, 12:13], SM2[:, 11:12]),
              waits=[ex], inc="v")
            o("vector", lambda e: e.tensor_single_scalar(EW[:], EW[:], SM2[:, 12:13],
                                                         ALU.mult), inc="v")
            smw = t("v")

            for k in range(TOPK):
                o("gpsimd", lambda e, k=k: e.indirect_dma_start(
                    out=GB1, out_offset=None, in_=down_embed,
                    in_offset=IndirectOffsetOnAxis(EIDI[:, k:k + 1], 0)),
                    waits=[eid_tok, prev_mult, mlp_tok], inc="g", dma=True)
                dg = t("g")
                o("vector", lambda e: e.tensor_mul(GB2, GB1, TB[:]),
                  waits=[dg, lntm_t, gb2_free], inc="v")
                prev_mult = t("v")
                em.vop(lambda e, k=k: e.reduce_sum(
                    SM2[:, 16 + k:17 + k].unsqueeze(1),
                    GB2.unsqueeze(1), AX.X),
                    waits=[prev_silw] if k == 0 else (), inc="v")
                gb2_free = t("v")
            dots = t("v")
            o("scalar", lambda e: e.activation(SM2[:, 24:32], SM2[:, 16:24], AF.Silu),
              waits=[dots], inc="s")
            silw = t("s")
            prev_silw = silw
            o("vector", lambda e: e.tensor_mul(SM2[:, 24:32], SM2[:, 24:32], EW[:]),
              waits=[silw, smw], inc="v")
            wfin = t("v")

            for k in range(TOPK):
                o("gpsimd", lambda e, k=k: e.indirect_dma_start(
                    out=GB1, out_offset=None, in_=up_embed,
                    in_offset=IndirectOffsetOnAxis(EIDI[:, k:k + 1], 0)),
                    waits=[prev_mult], inc="g", dma=True)
                ug = t("g")
                if k == 0:
                    o("vector", lambda e, k=k: e.tensor_single_scalar(
                        GB2, GB1, SM2[:, 24 + k:25 + k], ALU.mult),
                        waits=[ug, wfin, gb2_free], inc="v")
                else:
                    o("vector", lambda e, k=k: e.scalar_tensor_tensor(
                        GB2, GB1, SM2[:, 24 + k:25 + k], GB2, ALU.mult, ALU.add),
                        waits=[ug], inc="v")
                prev_mult = t("v")
            acc = t("v")
            for c in range(NCH):
                o("tensor", lambda e, c=c: e.transpose(
                    ps_big[:, c * 128:(c + 1) * 128], GB2[:, c * 128:(c + 1) * 128],
                    IDT), waits=[acc, pfree["big"]] if c == 0 else (),
                    inc="t" if c == NCH - 1 else None)
            tre = t("t")
            gb2_free = tre
            tb_free = dots
            for c in range(NCH):
                o("vector", lambda e, c=c, tile=tile: e.scalar_tensor_tensor(
                    XB[:, c, tile * 128:tile * 128 + 128],
                    ps_big[:, c * 128:(c + 1) * 128], 0.5,
                    XB[:, c, tile * 128:tile * 128 + 128], ALU.mult, ALU.add),
                    waits=[tre] if c == 0 else (), inc="v" if c == NCH - 1 else None)
            pfree["big"] = t("v")
        return t("v")

    def attention(XB, OUTB, xtok):
        LN2 = OUTB
        lntm, fm = None, None
        for tile in range(8):
            lntm, fm = ln_tile(XB, tile,
                               LN2.ap()[:, :, tile * 128:tile * 128 + 128], xtok,
                               ta_waits=[lntm] if lntm else (),
                               tb_waits=[fm] if fm else ())
        ln_tok = fm

        if DEBUG:
            o("gpsimd", lambda e: e.dma_start(
                out=DBG_LN2.rearrange("(c p) t -> p c t", p=128), in_=LN2.ap()),
                waits=[ln_tok], inc="g", dma=True)
            o("gpsimd", lambda e: e.dma_start(out=DBG_STAT, in_=STAT[:, :]),
                waits=[ln_tok], inc="g", dma=True)
        WSL = [WB.ap()[:, 0:2, :].rearrange("p a (c f) -> p (a c) f", f=128),
               WB.ap()[:, 2:4, :].rearrange("p a (c f) -> p (a c) f", f=128)]
        w_free = [xtok, xtok]
        stage_free = None

        def proj_loop(Wt, scale_ap, bias_ap, dst_fn):
            nonlocal stage_free
            for h in range(H_):
                buf = h % 2
                o("sync", lambda e, h=h, buf=buf, Wt=Wt: e.dma_start(
                    out=WSL[buf][:, 0:16, :],
                    in_=Wt[:, h * 128:(h + 1) * 128].rearrange("(c p) f -> p c f", p=128)),
                    waits=[w_free[buf]], inc="w", dma=True)
                wl = t("w")
                for half in range(2):
                    pn = "a" if half == 0 else "b"
                    psx = ps_a if half == 0 else ps_b
                    for c in range(NCH):
                        o("tensor", lambda e, c=c, buf=buf, hf=half, psx=psx: e.matmul(
                            psx[:, 0:512], WSL[buf][:, c, :],
                            LN2[:, c, hf * 512:(hf + 1) * 512],
                            start=(c == 0), stop=(c == NCH - 1)),
                            waits=[wl, ln_tok, pfree[pn]] if c == 0 else (),
                            inc="t" if c == NCH - 1 else None)
                    mm = t("t")
                    if half == 1:
                        w_free[buf] = mm
                    o("vector", lambda e, h=h, hf=half, psx=psx, sa=scale_ap,
                      ba=bias_ap: e.tensor_scalar(
                        TA[:, hf * 512:(hf + 1) * 512], psx[:, 0:512],
                        sa[:, h:h + 1], ba[:, h:h + 1], ALU.mult, ALU.add),
                        waits=[mm, scales_tok, stage_free], inc="v")
                    pfree[pn] = t("v")
                ev = t("v")
                dst_fn(h, ev)
                stage_free = t("g")

        def k_dst(h, ev):
            if DEBUG and h == 0:
                o("gpsimd", lambda e: e.dma_start(out=DBG_TA0, in_=TA[:]),
                  waits=[ev], inc="g", dma=True)
            o("gpsimd", lambda e, h=h: e.dma_start(
                out=K_loc[h * 128:(h + 1) * 128, :],
                in_=TA[:, 0:1024]), waits=[ev], inc="g", dma=True)
        proj_loop(WkT, KSC, KBE, k_dst)
        kw_loc = t("g")
        o("gpsimd", lambda e: e.dma_start(
            out=K_sh[bass.ds(snapv("slot", 0, 1) * D_, D_), :],
            in_=K_loc.bitcast(U32)), waits=[kw_loc], inc="g", dma=True)
        proj_loop(WqT, QSC, QBE, lambda h, ev: o(
            "gpsimd", lambda e, h=h: e.dma_start(
                out=Q_loc[h * 128:(h + 1) * 128, :], in_=TA[:, 0:1024]),
            waits=[ev], inc="g", dma=True))

        WV = WB.ap()[:, 0:4, :].rearrange("p a (c f) -> p (a c) f", f=256)
        vst_free = None
        last_v_mm = None
        for vs in range(8):
            o("sync", lambda e, vs=vs: e.dma_start(
                out=WV[:, 0:16, :],
                in_=WvT[:, vs * 256:(vs + 1) * 256].rearrange("(c p) f -> p c f", p=128)),
                waits=[w_free[0], w_free[1]] if vs == 0 else [last_v_mm],
                inc="w", dma=True)
            wvl = t("w")
            for tt_ in range(8):
                pn = "a" if tt_ % 2 == 0 else "b"
                psx = ps_a if tt_ % 2 == 0 else ps_b
                for c in range(NCH):
                    o("tensor", lambda e, c=c, tt_=tt_, psx=psx: e.matmul(
                        psx[:, 0:256], LN2[:, c, tt_ * 128:(tt_ + 1) * 128],
                        WV[:, c, :], start=(c == 0), stop=(c == NCH - 1)),
                        waits=[wvl, pfree[pn]] if c == 0 else (),
                        inc="t" if c == NCH - 1 else None)
                vmm = t("t")
                last_v_mm = vmm
                o("vector", lambda e, vs=vs, psx=psx: e.tensor_add(
                    SM3[:, 0:256], psx[:, 0:256], BVB[:, vs * 256:(vs + 1) * 256]),
                    waits=[vmm, bvb_tok, vst_free], inc="v")
                vadd = t("v")
                pfree[pn] = vadd
                o("gpsimd", lambda e, vs=vs, tt_=tt_: e.dma_start(
                    out=V_loc[tt_ * 128:(tt_ + 1) * 128, vs * 256:(vs + 1) * 256],
                    in_=SM3[:, 0:256]),
                    waits=[vadd], inc="g", dma=True)
                vst_free = t("g")
        vw_loc = t("g")
        o("gpsimd", lambda e: e.dma_start(
            out=V_sh[bass.ds(snapv("slot", 0, 1) * TPC, TPC), :],
            in_=V_loc.bitcast(U32)), waits=[vw_loc], inc="g", dma=True)
        kvw = t("g")

        o("gpsimd", lambda e: e.dma_start(
            out=X_stash.rearrange("(c p) t -> p c t", p=128), in_=XB.ap()),
            waits=[xtok], inc="g", dma=True)
        xst = t("g")
        o("gpsimd", lambda e: e.dma_start(out=bar_in, in_=BARZ),
            waits=[scales_tok], inc="g", dma=True)
        o("gpsimd", lambda e: e.collective_compute(
            "AllReduce", ALU.add, replica_groups=[[0, 1], [2, 3], [4, 5], [6, 7]],
            ins=[bar_in], outs=[bar_out]), waits=[kvw], inc="gc")
        kvready = t("gc")

        if DEBUG:
            o("gpsimd", lambda e: e.dma_start(out=DBG_KSH, in_=K_sh.bitcast(F32)),
              waits=[kvready], inc="g", dma=True)
            o("gpsimd", lambda e: e.dma_start(out=DBG_KLOC, in_=K_loc),
              waits=[kvready], inc="g", dma=True)
        if STOP_KV:
            return t("g")
        OBUF = XB
        PT = OUTB
        KH = [TB.ap(), WB.ap()[:, 0:2, :].rearrange("p a f -> p (a f)")]
        QH = [WB.ap()[:, 2, :], WB.ap()[:, 3, :]]
        VH = WB.ap()[:, 4:6, :].rearrange("p a (c f) -> p (a c) f", f=128)

        def load_kq(h, extra=()):
            buf = h % 2
            for s in range(2):
                o("gpsimd", lambda e, h=h, s=s, buf=buf: e.dma_start(
                    out=KH[buf][:, s * 1024:(s + 1) * 1024],
                    in_=K_sh[s * D_ + h * 128:s * D_ + (h + 1) * 128, :].bitcast(F32)),
                    waits=(list(extra) + [kvready]) if s == 0 else (),
                    inc="g", dma=True)
            o("gpsimd", lambda e, h=h, buf=buf: e.dma_start(
                out=QH[buf], in_=Q_loc[h * 128:(h + 1) * 128, :]), inc="g", dma=True)
            return t("g")

        def load_v(h, extra=()):
            for s in range(2):
                o("gpsimd", lambda e, h=h, s=s: e.dma_start(
                    out=VH[:, s * 8:(s + 1) * 8, :],
                    in_=V_sh[s * TPC:(s + 1) * TPC,
                             h * 128:(h + 1) * 128].bitcast(F32).rearrange(
                        "(c p) f -> p c f", p=128)),
                    waits=(list(extra) + [kvready]) if s == 0 else (),
                    inc="g", dma=True)
            return t("g")

        kq_tok = load_kq(0, extra=[stage_free, xst])
        prev_ptcopy = None
        prev_av = None
        prev_pexp = None
        for h in range(H_):
            buf = h % 2
            for qt_ in range(8):
                for tc in range(4):
                    o("tensor", lambda e, qt_=qt_, tc=tc, buf=buf: e.matmul(
                        ps_big[:, tc * 512:(tc + 1) * 512],
                        QH[buf][:, qt_ * 128:(qt_ + 1) * 128],
                        KH[buf][:, tc * 512:(tc + 1) * 512],
                        start=True, stop=True),
                        waits=[kq_tok, pfree["big"], ln_tok] if (qt_ == 0 and tc == 0)
                        else (), inc="t" if tc == 3 else None)
                smm = t("t")
                em.vop(lambda e: e.reduce_max(
                    SM2[:, 0:1].unsqueeze(1),
                    ps_big[:].unsqueeze(1), AX.X), waits=[smm, prev_pexp], inc="v")
                o("vector", lambda e: e.tensor_scalar_mul(SM2[:, 0:1], SM2[:, 0:1],
                                                          -1.0), inc="v")
                nmax = t("v")
                o("scalar", lambda e: e.activation(TA[:], ps_big[:], AF.Exp,
                                                   bias=SM2[:, 0:1],
                                                   accum_out=SM2[:, 1:2]),
                  waits=[nmax, prev_ptcopy], inc="s")
                pexp = t("s")
                prev_pexp = pexp
                pfree["big"] = pexp
                em.vop(lambda e: e.reciprocal(SM2[:, 2:3], SM2[:, 1:2]),
                  waits=[pexp], inc="v")
                o("vector", lambda e: e.tensor_single_scalar(TA[:], TA[:], SM2[:, 2:3],
                                                             ALU.mult), inc="v")
                pnorm = t("v")
                for g2 in range(2):
                    for tc in range(8):
                        o("tensor", lambda e, tc=tc, g2=g2: e.transpose(
                            ps_c[:, tc * 128:(tc + 1) * 128],
                            TA[:, (g2 * 8 + tc) * 128:(g2 * 8 + tc + 1) * 128], IDT),
                            waits=[pnorm, pfree["c"]] if tc == 0 else (),
                            inc="t" if tc == 7 else None)
                    trp = t("t")
                    o("vector", lambda e, qt_=qt_, g2=g2: e.tensor_copy(
                        PT.ap()[:, g2 * 8:(g2 + 1) * 8, qt_ * 128:(qt_ + 1) * 128],
                        ps_c[:].rearrange("p (c f) -> p c f", c=8)),
                        waits=[trp, prev_av], inc="v")
                    prev_ptcopy = t("v")
                    pfree["c"] = prev_ptcopy
            pt_tok = prev_ptcopy
            v_tok = load_v(h, extra=[prev_av])
            kq_next = load_kq(h + 1, extra=[t("t")]) if h + 1 < H_ else None
            for qs in range(2):
                pn = "a" if qs == 0 else "b"
                psx = ps_a if qs == 0 else ps_b
                for tc in range(16):
                    o("tensor", lambda e, qs=qs, tc=tc, psx=psx: e.matmul(
                        psx[:, 0:512], VH[:, tc, :],
                        PT.ap()[:, tc, qs * 512:(qs + 1) * 512],
                        start=(tc == 0), stop=(tc == 15)),
                        waits=[pt_tok, v_tok, pfree[pn]] if tc == 0 else (),
                        inc="t" if tc == 15 else None)
                avm = t("t")
                o("vector", lambda e, h=h, qs=qs, psx=psx: e.tensor_copy(
                    OBUF[:, h, qs * 512:(qs + 1) * 512], psx[:, 0:512]),
                    waits=[avm, xst], inc="v")
                pfree[pn] = t("v")
            prev_av = t("v")
            if kq_next is not None:
                kq_tok = kq_next
        o_tok = prev_av
        if F_O:
            o("gpsimd", lambda e: e.dma_start(
                out=LN_fm.rearrange("(c p) t -> p c t", p=128), in_=OBUF.ap()),
                waits=[o_tok], inc="g", dma=True)
            o_tok = t("g")
        if DEBUG:
            o("gpsimd", lambda e: e.dma_start(
                out=DBG_O.rearrange("(c p) t -> p c t", p=128), in_=OBUF.ap()),
                waits=[o_tok], inc="g", dma=True)
            o_tok = t("g")

        WOB = [WB.ap()[:, 0:2, :].rearrange("p a (c f) -> p (a c) f", f=128),
               WB.ap()[:, 2:4, :].rearrange("p a (c f) -> p (a c) f", f=128)]
        wo_free = [o_tok, o_tok]
        xres_free = None
        for i in range(NCH):
            buf = i % 2
            o("sync", lambda e, i=i, buf=buf: e.dma_start(
                out=WOB[buf][:, 0:16, :],
                in_=WoT[:, i * 128:(i + 1) * 128].rearrange("(c p) f -> p c f", p=128)),
                waits=[wo_free[buf]], inc="w", dma=True)
            wl = t("w")
            o("gpsimd", lambda e, i=i: e.dma_start(
                out=TA[:, 0:1024], in_=X_stash[i * 128:(i + 1) * 128, :]),
                waits=[xst, xres_free, o_tok], inc="g", dma=True)
            xld = t("g")
            for half in range(2):
                pn = "a" if half == 0 else "b"
                psx = ps_a if half == 0 else ps_b
                for c in range(NCH):
                    o("tensor", lambda e, c=c, buf=buf, hf=half, psx=psx: e.matmul(
                        psx[:, 0:512], WOB[buf][:, c, :],
                        OBUF[:, c, hf * 512:(hf + 1) * 512],
                        start=(c == 0), stop=(c == NCH - 1)),
                        waits=[wl, o_tok, pfree[pn]] if c == 0 else (),
                        inc="t" if c == NCH - 1 else None)
                omm = t("t")
                if half == 1:
                    wo_free[buf] = omm
                o("vector", lambda e, i=i, hf=half, psx=psx: e.tensor_scalar_add(
                    TB[:, hf * 512:(hf + 1) * 512], psx[:, 0:512],
                    BOc[:, i:i + 1]), waits=[omm], inc="v")
                ob = t("v")
                pfree[pn] = ob
                o("vector", lambda e, i=i, hf=half: e.tensor_add(
                    OUTB[:, i, hf * 512:(hf + 1) * 512],
                    TB[:, hf * 512:(hf + 1) * 512], TA[:, hf * 512:(hf + 1) * 512]),
                    waits=[ob, xld], inc="v")
            xres_free = t("v")
        return t("v")

    tok0 = ffn(0, XT0, AR, ldx)
    if DEBUG:
        o("gpsimd", lambda e: e.dma_start(
            out=DBG_X1.rearrange("(c p) t -> p c t", p=128), in_=XT0.ap()),
            waits=[tok0], inc="g", dma=True)
    tok1 = attention(XT0, AR, tok0)
    if STOP_KV:
        o("gpsimd", lambda e: e.dma_start(
            out=yT.rearrange("(c p) t -> p c t", p=128), in_=XT0.ap()),
            waits=[tok1], inc="g", dma=True)
        o("gpsimd", lambda e: e.nop(), waits=[t("g")])
        em.emit()
        return nc
    if DEBUG:
        o("gpsimd", lambda e: e.dma_start(
            out=DBG_X2.rearrange("(c p) t -> p c t", p=128), in_=AR.ap()),
            waits=[tok1], inc="g", dma=True)
    tok2 = ffn(1, AR, XT0, tok1)

    o("gpsimd", lambda e: e.dma_start(out=yT.rearrange("(c p) t -> p c t", p=128),
                                      in_=AR.ap()), waits=[tok2], inc="g", dma=True)
    fin = t("g")
    o("gpsimd", lambda e: e.nop(), waits=[fin])

    em.emit()
    return nc


_NC_CACHE = {}


def kernel(L0_attn_x, L0_attn_Wq, L0_attn_bq, L0_attn_Wk, L0_attn_bk,
           L0_attn_Wv, L0_attn_bv, L0_attn_Wo, L0_attn_bo,
           L0_attn_spiral_weight, L0_attn_helix_rotation,
           L0_ffn_W_q, L0_ffn_keys, L0_ffn_down_embed, L0_ffn_up_embed,
           L0_ffn_gate_w, L0_ffn_up_w, L0_ffn_down_w):
    x = np.ascontiguousarray(np.asarray(L0_attn_x, dtype=np.float32).reshape(NTOK, D_))
    WqT = np.ascontiguousarray(np.asarray(L0_attn_Wq, np.float32).T)
    WkT = np.ascontiguousarray(np.asarray(L0_attn_Wk, np.float32).T)
    WvT = np.ascontiguousarray(np.asarray(L0_attn_Wv, np.float32).T)
    WoT = np.ascontiguousarray(np.asarray(L0_attn_Wo, np.float32).T)
    keys = np.asarray(L0_ffn_keys, np.float32)

    if "nc" not in _NC_CACHE:
        _NC_CACHE["nc"] = build_megakernel()
    nc = _NC_CACHE["nc"]

    in_maps = []
    for c in range(N_CORES):
        in_maps.append({
            "xT": np.ascontiguousarray(x[c * TPC:(c + 1) * TPC].T),
            "WqT": WqT, "WkT": WkT, "WvT": WvT, "WoT": WoT,
            "bq": np.asarray(L0_attn_bq, np.float32),
            "bk": np.asarray(L0_attn_bk, np.float32),
            "bv": np.asarray(L0_attn_bv, np.float32),
            "bo": np.asarray(L0_attn_bo, np.float32),
            "spiral": np.asarray(L0_attn_spiral_weight, np.float32),
            "helix": np.asarray(L0_attn_helix_rotation, np.float32),
            "W_q": np.asarray(L0_ffn_W_q, np.float32),
            "keys_mine": np.ascontiguousarray(keys[0 if c < 4 else 1]),
            "down_embed": np.asarray(L0_ffn_down_embed, np.float32),
            "up_embed": np.asarray(L0_ffn_up_embed, np.float32),
            "gate_w": np.asarray(L0_ffn_gate_w, np.float32),
            "up_w": np.asarray(L0_ffn_up_w, np.float32),
            "down_w": np.asarray(L0_ffn_down_w, np.float32),
            "agoff": np.array([[(c // 2) * 2 * TPC + (c % 2) * TPC,
                                (4 + c // 2) * 2 * TPC + (c % 2) * TPC]],
                              dtype=np.int32),
            "slot": np.array([[c % 2]], dtype=np.int32),
        })

    res = run_bass_kernel_spmd(nc, in_maps, core_ids=list(range(N_CORES)))
    out = np.empty((NTOK, D_), dtype=np.float32)
    for c in range(N_CORES):
        out[c * TPC:(c + 1) * TPC] = res.results[c]["yT"].T
    return out.reshape(B_, S_, D_)


# version 07 (final)
# speedup vs baseline: 1.1106x; 1.1106x over previous
import numpy as np
import concourse.bass as bass
import concourse.mybir as mybir
from concourse.bass_utils import run_bass_kernel_spmd
from concourse.bass_core_rust import IndirectOffsetOnAxis

F32 = mybir.dt.float32
U32 = mybir.dt.uint32
I32 = mybir.dt.int32
AF = mybir.ActivationFunctionType
ALU = mybir.AluOpType
AX = mybir.AxisListType
EngT = mybir.EngineType

N_CORES = 8
B_, S_, D_, H_ = 4, 2048, 2048, 16
DH = 128
NTOK = 8192
TPC = 1024
NCH = 16
INTER = 8192
TOPK = 8
EPS = 1e-5
BIGNEG = -1.0e30
PI = float(np.pi)

DEBUG = False
STOP_KV = False
F_LNF = False
F_SIL = False
F_H = False
F_O = False
ENGINES = ("sync", "gpsimd", "tensor", "vector", "scalar")
OWN_SEM = {"vector": "v", "scalar": "s", "tensor": "t"}


class Emitter:

    def __init__(self, nc):
        self.nc = nc
        self.streams = {e: [] for e in ENGINES}
        self.sems = {}
        self.cnt = {}

    def sem(self, name):
        if name not in self.sems:
            self.sems[name] = self.nc.alloc_semaphore(name)
            self.cnt[name] = 0
        return self.sems[name]

    def op(self, eng, make, waits=(), inc=None, dma=False):
        own = OWN_SEM.get(eng)
        ws = []
        for w in waits:
            if w is None:
                continue
            s, v = w
            if s == own:
                continue
            assert v <= self.cnt.get(s, 0), f"future token {s}={v}>{self.cnt.get(s, 0)}"
            ws.append((s, v))
        if eng == "vector" and inc is None:
            inc = "v"
        if inc is not None:
            self.sem(inc)
            self.cnt[inc] += 16 if dma else 1
        self.streams[eng].append((make, tuple(ws), inc, dma))
        if eng == "vector":
            self.streams[eng].append((None, ((inc, self.cnt[inc]),), None, False))

    def vop(self, make, waits=(), inc="v"):
        self.op("vector", make, waits=waits, inc=inc)

    def tok(self, name):
        return (name, self.cnt.get(name, 0))

    def emit(self):
        with self.nc.Block() as block:
            for e in ENGINES:
                stream = self.streams[e]

                def body(engine, stream=stream):
                    last = {}
                    for make, waits, inc, dma in stream:
                        if make is None:
                            for s, v in waits:
                                engine.wait_ge(self.sems[s], v)
                                last[s] = max(last.get(s, -1), v)
                            continue
                        for s, v in waits:
                            if last.get(s, -1) < v:
                                engine.wait_ge(self.sems[s], v)
                                last[s] = v
                        ins = make(engine)
                        if inc is not None:
                            ins.then_inc(self.sems[inc], 16 if dma else 1)
                getattr(block, e)(body)


def build_megakernel():
    nc = bass.Bass("TRN2", target_bir_lowering=False, num_devices=N_CORES,
                   detect_race_conditions=False)
    em = Emitter(nc)
    o, t = em.op, em.tok

    def din(name, shape, dtype=F32):
        return nc.dram_tensor(name, list(shape), dtype, kind="ExternalInput").ap()

    xT = din("xT", [D_, TPC])
    WqT = din("WqT", [D_, D_]); WkT = din("WkT", [D_, D_])
    WvT = din("WvT", [D_, D_]); WoT = din("WoT", [D_, D_])
    bq = din("bq", [D_]); bk = din("bk", [D_]); bv = din("bv", [D_]); bo = din("bo", [D_])
    spiral = din("spiral", [H_]); helix = din("helix", [D_])
    W_q = din("W_q", [D_, 128]); keys_mine = din("keys_mine", [64, 64])
    down_embed = din("down_embed", [4096, D_]); up_embed = din("up_embed", [4096, D_])
    gate_w = din("gate_w", [D_, INTER]); up_w = din("up_w", [D_, INTER])
    down_w = din("down_w", [INTER, D_])
    agoff = din("agoff", [1, 2], I32)
    slot_in = din("slot", [1, 1], I32)
    yT = nc.dram_tensor("yT", [D_, TPC], F32, kind="ExternalOutput").ap()
    ident_d = nc.inline_tensor(np.eye(128, dtype=np.float32), "ident")

    K_sh = nc.dram_tensor("K_sh", [2 * D_, TPC], U32, addr_space="Shared").ap()
    V_sh = nc.dram_tensor("V_sh", [2 * TPC, D_], U32, addr_space="Shared").ap()
    Q_loc = nc.dram_tensor("Q_loc", [D_, TPC], F32,
                           kind="ExternalOutput" if DEBUG else "Internal").ap()
    K_loc = nc.dram_tensor("K_loc", [D_, TPC], F32).ap()
    V_loc = nc.dram_tensor("V_loc", [TPC, D_], F32).ap()
    X_stash = nc.dram_tensor("X_stash", [D_, TPC], F32).ap()
    LN_fm = nc.dram_tensor("LN_fm", [D_, TPC], F32).ap()
    LN_tm = nc.dram_tensor("LN_tm", [TPC, D_], F32).ap()
    if DEBUG:
        DBG_X1 = nc.dram_tensor("DBG_X1", [D_, TPC], F32, kind="ExternalOutput").ap()
        DBG_X2 = nc.dram_tensor("DBG_X2", [D_, TPC], F32, kind="ExternalOutput").ap()
        DBG_LNTM = nc.dram_tensor("DBG_LNTM", [TPC, D_], F32, kind="ExternalOutput").ap()
        DBG_S8 = nc.dram_tensor("DBG_S8", [128, 16, 8], F32, kind="ExternalOutput").ap()
        DBG_I8 = nc.dram_tensor("DBG_I8", [128, 16, 8], F32, kind="ExternalOutput").ap()
        DBG_SX = nc.dram_tensor("DBG_SX", [128, 8, 16], F32, kind="ExternalOutput").ap()
        DBG_SY = nc.dram_tensor("DBG_SY", [128, 8, 16], F32, kind="ExternalOutput").ap()
        DBG_QT = nc.dram_tensor("DBG_QT", [128, 16, 64], F32, kind="ExternalOutput").ap()
        DBG_QLO = nc.dram_tensor("DBG_QLO", [64, 1024], F32, kind="ExternalOutput").ap()
        DBG_MLP = nc.dram_tensor("DBG_MLP", [D_, TPC], F32, kind="ExternalOutput").ap()
        DBG_H = nc.dram_tensor("DBG_H", [128, 8, 512], F32, kind="ExternalOutput").ap()
        DBG_LNF = nc.dram_tensor("DBG_LNF", [128, 16, 512], F32, kind="ExternalOutput").ap()
        DBG_SIL = nc.dram_tensor("DBG_SIL", [128, 512], F32, kind="ExternalOutput").ap()
        DBG_UP = nc.dram_tensor("DBG_UP", [128, 512], F32, kind="ExternalOutput").ap()
        DBG_LNFM_RAW = nc.dram_tensor("DBG_LNFM_RAW", [D_, TPC], F32, kind="ExternalOutput").ap()
        DBG_O = nc.dram_tensor("DBG_O", [D_, TPC], F32, kind="ExternalOutput").ap()
        DBG_KSH = nc.dram_tensor("DBG_KSH", [2 * D_, TPC], F32, kind="ExternalOutput").ap()
        DBG_KLOC = nc.dram_tensor("DBG_KLOC", [D_, TPC], F32, kind="ExternalOutput").ap()
        DBG_CONST = nc.dram_tensor("DBG_CONST", [128, 448], F32, kind="ExternalOutput").ap()
        DBG_TA0 = nc.dram_tensor("DBG_TA0", [128, 2048], F32, kind="ExternalOutput").ap()
        DBG_LN2 = nc.dram_tensor("DBG_LN2", [D_, TPC], F32, kind="ExternalOutput").ap()
        DBG_STAT = nc.dram_tensor("DBG_STAT", [128, 24], F32, kind="ExternalOutput").ap()
    bar_in = nc.dram_tensor("bar_in", [1, 4], F32).ap()
    bar_out = nc.dram_tensor("bar_out", [1, 4], F32).ap()
    ag_in = [nc.dram_tensor(f"ag_in{i}", [2 * TPC, 16], F32).ap() for i in range(2)]
    ag_out = [nc.dram_tensor(f"ag_out{i}", [N_CORES * 2 * TPC, 16], F32).ap()
              for i in range(2)]

    sb = nc.alloc_sbuf_tensor
    XT0 = sb("XT0", [128, NCH, TPC], F32)
    AR = sb("AR", [128, NCH, TPC], F32)
    WB = sb("WB", [128, 8, TPC], F32)
    TA = sb("TA", [128, 2048], F32)
    TB = sb("TB", [128, 2048], F32)
    SIL = sb("SIL", [128, 512], F32)
    SM = sb("SM", [128, 1024], F32)
    SM2 = sb("SM2", [128, 1024], F32)
    SM3 = sb("SM3", [128, 1024], F32)
    QLO = sb("QLO", [64, 1024], F32)
    QHI = sb("QHI", [64, 1024], F32)
    QT = sb("QT", [128, 16, 64], F32)
    S8 = sb("S8", [128, 16, 8], F32)
    I8 = sb("I8", [128, 16, 8], F32)
    SX8 = sb("SX8", [128, 8, 16], F32)
    SY8 = sb("SY8", [128, 8, 16], F32)
    CMB = sb("CMB", [128, 64], F32)
    CMI = sb("CMI", [128, 64], F32)
    EW = sb("EW", [128, 8], F32)
    EIDI = sb("EIDI", [128, 8], I32)
    STAT = sb("STAT", [128, 24], F32)
    CONST = sb("CONST", [128, 448], F32)
    IDT = CONST[:, 0:128]
    ONES = CONST[:, 128:129]
    QSC = CONST[:, 130:146]
    KSC = CONST[:, 146:162]
    BQc = CONST[:, 162:178]
    BKc = CONST[:, 178:194]
    BOc = CONST[:, 194:210]
    QBE = CONST[:, 210:226]
    KBE = CONST[:, 226:242]
    IOTA = CONST[:, 242:306]
    ONESR = CONST[0:1, 312:440]
    BARZ = CONST[0:1, 440:444]
    SL = sb("SL", [1, 8], I32)

    ps_big = nc.alloc_psum_tensor("ps_big", [128, 2048], F32)
    ps_c = nc.alloc_psum_tensor("ps_c", [128, 1024], F32)
    ps_a = nc.alloc_psum_tensor("ps_a", [128, 512], F32)
    ps_b = nc.alloc_psum_tensor("ps_b", [128, 512], F32)
    assert nc.sbuf_bytes_remaining >= 0, nc.sbuf_bytes_remaining

    pfree = {"big": None, "c": None, "a": None, "b": None}

    o("sync", lambda e: e.dma_start(out=XT0.ap(),
                                    in_=xT.rearrange("(c p) t -> p c t", p=128)),
      inc="w", dma=True)
    ldx = t("w")
    o("sync", lambda e: e.dma_start(out=IDT, in_=ident_d.ap()), inc="w", dma=True)
    o("sync", lambda e: e.dma_start(out=TA[0:16, 0:128],
                                    in_=helix.rearrange("(h d) -> h d", h=16)),
      inc="w", dma=True)
    o("sync", lambda e: e.dma_start(out=TA[0:16, 128:129],
                                    in_=spiral.unsqueeze(1)), inc="w", dma=True)
    for i, src in enumerate((bq, bk, bo)):
        o("sync", lambda e, s=src, i=i: e.dma_start(
            out=TA[0:16, 512 + 128 * i:640 + 128 * i],
            in_=s.rearrange("(c d) -> c d", c=16)), inc="w", dma=True)
    o("sync", lambda e: e.dma_start(out=SL[0:1, 0:1], in_=slot_in), inc="w", dma=True)
    o("sync", lambda e: e.dma_start(out=SL[0:1, 2:4], in_=agoff), inc="w", dma=True)
    o("sync", lambda e: e.dma_start(out=TB[0:1, :], in_=bv.unsqueeze(0)),
      inc="w", dma=True)
    ldc = t("w")

    o("vector", lambda e: e.memset(ONES, 1.0), waits=[ldc], inc="v")
    o("vector", lambda e: e.memset(ONESR, 1.0), inc="v")
    o("vector", lambda e: e.memset(STAT[:, 20:21], float(EPS)), inc="v")
    o("vector", lambda e: e.memset(BARZ, 0.0), inc="v")
    o("gpsimd", lambda e: e.iota(IOTA, [[1, 64]], channel_multiplier=0,
                                 allow_small_or_imprecise_dtypes=True),
      waits=[ldc], inc="gc")
    iota_tok = t("gc")

    def wrap_sin(dst, shift):
        ga, gb = SM[0:16, 0:128], SM[0:16, 128:256]
        o("vector", lambda e: e.tensor_single_scalar(ga, TA[0:16, 0:128], PI - shift,
                                                     ALU.is_gt), waits=[ldc], inc="v")
        o("vector", lambda e: e.tensor_single_scalar(gb, TA[0:16, 0:128], -PI - shift,
                                                     ALU.is_lt), inc="v")
        o("vector", lambda e: e.tensor_single_scalar(dst, TA[0:16, 0:128], shift,
                                                     ALU.add), inc="v")
        o("vector", lambda e: e.scalar_tensor_tensor(dst, ga, -2 * PI, dst,
                                                     ALU.mult, ALU.add), inc="v")
        o("vector", lambda e: e.scalar_tensor_tensor(dst, gb, 2 * PI, dst,
                                                     ALU.mult, ALU.add), inc="v")
        wv = t("v")
        o("scalar", lambda e: e.activation(dst, dst, AF.Sin), waits=[wv], inc="s")
        return t("s")

    karg = TA[0:16, 256:384]
    qarg = TA[0:16, 384:512]
    ks_tok = wrap_sin(karg, 0.0)
    qs_tok = wrap_sin(qarg, PI / 2)
    o("vector", lambda e: e.tensor_scalar_mul(TA[0:16, 128:129], TA[0:16, 128:129],
                                              1.0 / float(np.sqrt(DH))),
      waits=[ldc], inc="v")
    o("vector", lambda e: e.tensor_single_scalar(qarg, qarg, TA[0:16, 128:129],
                                                 ALU.mult), waits=[qs_tok], inc="v")
    sc_rows = t("v")
    o("tensor", lambda e: e.transpose(ps_a[:, 0:16], karg, IDT[0:16, 0:16]),
      waits=[sc_rows, ks_tok, ldc], inc="t")
    o("tensor", lambda e: e.transpose(ps_a[:, 16:32], qarg, IDT[0:16, 0:16]), inc="t")
    for i in range(3):
        o("tensor", lambda e, i=i: e.transpose(
            ps_a[:, 32 + 16 * i:48 + 16 * i], TA[0:16, 512 + 128 * i:640 + 128 * i],
            IDT[0:16, 0:16]), inc="t")
    trs = t("t")
    o("vector", lambda e: e.tensor_copy(KSC, ps_a[:, 0:16]), waits=[trs], inc="v")
    o("vector", lambda e: e.tensor_copy(QSC, ps_a[:, 16:32]), inc="v")
    o("vector", lambda e: e.tensor_copy(BQc, ps_a[:, 32:48]), inc="v")
    o("vector", lambda e: e.tensor_copy(BKc, ps_a[:, 48:64]), inc="v")
    o("vector", lambda e: e.tensor_copy(BOc, ps_a[:, 64:80]), inc="v")
    o("vector", lambda e: e.tensor_mul(QBE, QSC, BQc), inc="v")
    o("vector", lambda e: e.tensor_mul(KBE, KSC, BKc), inc="v")
    scales_tok = t("v")
    pfree["a"] = scales_tok

    if DEBUG:
        o("gpsimd", lambda e: e.dma_start(out=DBG_CONST, in_=CONST[:, :]),
          waits=[scales_tok], inc="g", dma=True)
    BVB = WB.ap()[:, 6:8, :].rearrange("p a f -> p (a f)")
    for q in range(4):
        o("tensor", lambda e, q=q: e.matmul(
            ps_big[:, q * 512:(q + 1) * 512], ONESR,
            TB[0:1, q * 512:(q + 1) * 512], start=True, stop=True),
            waits=[ldc, t("v")] if q == 0 else (), inc="t" if q == 3 else None)
    bvm = t("t")
    o("vector", lambda e: e.tensor_copy(BVB, ps_big[:]), waits=[bvm], inc="v")
    bvb_tok = t("v")
    pfree["big"] = bvb_tok

    regv = {}

    def mkreg(e, name, apsl):
        r = e.alloc_register(name)
        regv[name + "_reg"] = r
        return e.reg_load(r, apsl)

    o("gpsimd", lambda e: mkreg(e, "slot", SL[0:1, 0:1]), waits=[ldc])
    o("gpsimd", lambda e: mkreg(e, "offsx", SL[0:1, 2:3]), waits=[ldc])
    o("gpsimd", lambda e: mkreg(e, "offsy", SL[0:1, 3:4]), waits=[ldc])

    def snapv(name, minv, maxv):
        key = name + "_snap"
        if key not in regv:
            regv[key] = nc.snap(regv[name + "_reg"], engines=[EngT.Pool],
                                min_val=minv, max_val=maxv)
        return regv[key]

    def ln_tile(XB, tile, dst_fm, xtok, ta_waits=(), tb_waits=()):
        tb = tile * 128
        for c in range(NCH):
            o("tensor", lambda e, c=c: e.transpose(
                ps_big[:, c * 128:(c + 1) * 128], XB[:, c, tb:tb + 128], IDT),
                waits=[xtok, ldc, pfree["big"]] if c == 0 else (),
                inc="t" if c == NCH - 1 else None)
        tr1 = t("t")
        o("vector", lambda e: e.tensor_copy(TA[:], ps_big[:]),
          waits=[tr1] + list(ta_waits), inc="v")
        xtm = t("v")
        pfree["big"] = xtm
        em.vop(lambda e: e.reduce_sum(
            STAT[:, 16:17].unsqueeze(1),
            TA[:].unsqueeze(1), AX.X), inc="v")
        o("scalar", lambda e: e.activation(TB[:], TA[:], AF.Square),
          waits=[xtm] + list(tb_waits), inc="s")
        sq = t("s")
        em.vop(lambda e: e.reduce_sum(
            STAT[:, 17:18].unsqueeze(1),
            TB[:].unsqueeze(1), AX.X), waits=[sq], inc="v")
        o("vector", lambda e: e.tensor_scalar_mul(STAT[:, 16:17], STAT[:, 16:17],
                                                  1.0 / D_), inc="v")
        o("vector", lambda e: e.tensor_scalar_mul(STAT[:, 17:18], STAT[:, 17:18],
                                                  1.0 / D_), inc="v")
        o("vector", lambda e: e.tensor_mul(STAT[:, 18:19], STAT[:, 16:17],
                                           STAT[:, 16:17]), inc="v")
        o("vector", lambda e: e.tensor_sub(STAT[:, 17:18], STAT[:, 17:18],
                                           STAT[:, 18:19]), inc="v")
        o("vector", lambda e: e.tensor_scalar_add(STAT[:, 17:18], STAT[:, 17:18],
                                                  float(EPS)), inc="v")
        vv = t("v")
        o("scalar", lambda e: e.activation(STAT[:, 19:20], STAT[:, 17:18], AF.Sqrt),
          waits=[vv], inc="s")
        sq2 = t("s")
        rcol = STAT[:, 2 * tile:2 * tile + 1]
        bcol = STAT[:, 2 * tile + 1:2 * tile + 2]
        em.vop(lambda e: e.reciprocal(rcol, STAT[:, 19:20]), waits=[sq2], inc="v")
        for _newton in range(2):
            o("vector", lambda e: e.tensor_mul(STAT[:, 21:22], rcol, rcol), inc="v")
            o("vector", lambda e: e.tensor_mul(STAT[:, 21:22], STAT[:, 21:22],
                                               STAT[:, 17:18]), inc="v")
            o("vector", lambda e: e.tensor_scalar(STAT[:, 21:22], STAT[:, 21:22],
                                                  3.0, -0.5, ALU.subtract, ALU.mult),
              inc="v")
            o("vector", lambda e: e.tensor_mul(rcol, rcol, STAT[:, 21:22]), inc="v")
        o("vector", lambda e: e.scalar_tensor_tensor(bcol, STAT[:, 16:17], -1.0, rcol,
                                                     ALU.mult, ALU.mult), inc="v")
        stats = t("v")
        o("vector", lambda e: e.tensor_scalar(TB[:], TA[:], rcol, bcol,
                                              ALU.mult, ALU.add),
          waits=[stats], inc="v")
        lntm = t("v")
        for c in range(NCH):
            o("tensor", lambda e, c=c: e.transpose(
                ps_big[:, c * 128:(c + 1) * 128], TB[:, c * 128:(c + 1) * 128], IDT),
                waits=[lntm, pfree["big"]] if c == 0 else (),
                inc="t" if c == NCH - 1 else None)
        tr2 = t("t")
        dst = dst_fm if dst_fm is not None else TA[:].rearrange("p (c t) -> p c t", c=NCH)
        o("vector", lambda e: e.tensor_copy(
            dst, ps_big[:].rearrange("p (c t) -> p c t", c=NCH)), waits=[tr2], inc="v")
        fm = t("v")
        pfree["big"] = fm
        return lntm, fm

    def ffn(call, XB, SC, xtok):
        SCv = SC.ap().rearrange("p c (a f) -> p (c a) f", a=2)
        HSL = [SCv[:, 16 + i, :] for i in range(8)]
        GB1 = SC.ap()[:, 12:14, :].rearrange("p a f -> p (a f)")
        GB2 = SC.ap()[:, 14:16, :].rearrange("p a f -> p (a f)")
        agi, ago = ag_in[call], ag_out[call]

        WGU = [WB.ap()[:, 0:2, :].rearrange("p a (c f) -> p (a c) f", f=128),
               WB.ap()[:, 2:4, :].rearrange("p a (c f) -> p (a c) f", f=128),
               WB.ap()[:, 6:8, :].rearrange("p a (c f) -> p (a c) f", f=128)]
        WDN = [WB.ap()[:, 4:5, :].rearrange("p a (c f) -> p (a c) f", f=128),
               WB.ap()[:, 5:6, :].rearrange("p a (c f) -> p (a c) f", f=128)]
        gu_free = [xtok, xtok, xtok]
        dn_free = [xtok, xtok]

        o("sync", lambda e: e.dma_start(
            out=WGU[0][:, 0:16, :], in_=W_q.rearrange("(c p) f -> p c f", p=128)),
            waits=[gu_free[0]], inc="w", dma=True)
        wq_l = t("w")
        guard_ta = []
        guard_tb = []
        for tile in range(8):
            lntm, fm = ln_tile(XB, tile, None, xtok,
                               ta_waits=guard_ta, tb_waits=guard_tb)
            o("gpsimd", lambda e, tile=tile: e.dma_start(
                out=LN_tm[tile * 128:(tile + 1) * 128, :], in_=TB[:]),
                waits=[lntm], inc="g", dma=True)
            tm_st = t("g")
            o("gpsimd", lambda e, tile=tile: e.dma_start(
                out=LN_fm.rearrange("(c p) t -> p c t", p=128)[:, :,
                                                              tile * 128:(tile + 1) * 128],
                in_=TA[:].rearrange("p (c t) -> p c t", c=NCH)),
                waits=[fm], inc="g", dma=True)
            fm_st = t("g")
            for g, (qdst, psx, pn) in enumerate(((QLO, ps_a, "a"), (QHI, ps_b, "b"))):
                for c in range(NCH):
                    o("tensor", lambda e, c=c, g=g, psx=psx: e.matmul(
                        psx[0:64, 0:128], WGU[0][:, c, g * 64:(g + 1) * 64],
                        TA[:, c * 128:(c + 1) * 128],
                        start=(c == 0), stop=(c == NCH - 1)),
                        waits=[wq_l, fm, pfree[pn]] if c == 0 else (),
                        inc="t" if c == NCH - 1 else None)
                qmm = t("t")
                o("vector", lambda e, tile=tile, qdst=qdst, psx=psx: e.tensor_copy(
                    qdst[:, tile * 128:(tile + 1) * 128], psx[0:64, 0:128]),
                    waits=[qmm], inc="v")
                pfree[pn] = t("v")
            guard_ta = [fm_st, qmm]
            guard_tb = [tm_st]
        ln_all = t("g")
        gu_free[0] = t("t")
        if DEBUG and call == 0:
            o("gpsimd", lambda e: e.dma_start(out=DBG_LNTM, in_=LN_tm),
              waits=[ln_all], inc="g", dma=True)
            o("gpsimd", lambda e: e.dma_start(out=DBG_LNFM_RAW, in_=LN_fm),
              waits=[ln_all], inc="g", dma=True)

        o("sync", lambda e: e.dma_start(out=WGU[1][0:64, 0, 0:64], in_=keys_mine),
          waits=[gu_free[1]], inc="w", dma=True)
        keysl = t("w")
        qfin = t("v")
        for half in range(2):
            o("tensor", lambda e, half=half: e.matmul(
                ps_c[0:64, half * 512:(half + 1) * 512], WGU[1][0:64, 0, 0:64],
                QLO[:, half * 512:(half + 1) * 512], start=True, stop=True),
                waits=[keysl, qfin, pfree["c"]] if half == 0 else (),
                inc="t" if half == 1 else None)
        rmm0 = t("t")
        for half in range(2):
            o("tensor", lambda e, half=half: e.matmul(
                ps_big[0:64, half * 512:(half + 1) * 512], WGU[1][0:64, 0, 0:64],
                QHI[:, half * 512:(half + 1) * 512], start=True, stop=True),
                waits=[pfree["big"]] if half == 0 else (),
                inc="t" if half == 1 else None)
        rmm = t("t")
        gu_free[1] = rmm
        o("vector", lambda e: e.tensor_copy(SM3[0:64, :], ps_c[0:64, :]),
          waits=[rmm0], inc="v")
        r0ev = t("v")
        pfree["c"] = r0ev
        o("vector", lambda e: e.tensor_copy(SM[0:64, :], ps_big[0:64, 0:1024]),
          waits=[rmm], inc="v")
        r1ev = t("v")
        pfree["big"] = r1ev
        for p in range(2):
            srcbuf = SM3 if p == 0 else SM
            for jb in range(8):
                pn = "a" if (p * 8 + jb) % 2 == 0 else "b"
                psx = ps_a if (p * 8 + jb) % 2 == 0 else ps_b
                o("tensor", lambda e, p=p, jb=jb, psx=psx, srcbuf=srcbuf: e.transpose(
                    psx[:, 0:64], srcbuf[0:64, jb * 128:(jb + 1) * 128],
                    IDT[0:64, 0:64]),
                    waits=[r0ev if p == 0 else r1ev, pfree[pn]], inc="t")
                trq = t("t")
                o("vector", lambda e, p=p, jb=jb, psx=psx: e.tensor_copy(
                    QT[:, p * 8 + jb, :], psx[:, 0:64]), waits=[trq], inc="v")
                pfree[pn] = t("v")
        qt_tok = t("v")
        if DEBUG and call == 0:
            o("gpsimd", lambda e: e.dma_start(out=DBG_QT, in_=QT.ap()),
              waits=[qt_tok], inc="g", dma=True)
            o("gpsimd", lambda e: e.dma_start(out=DBG_QLO, in_=QLO.ap()),
              waits=[qt_tok], inc="g", dma=True)
        qt_dump = t("g")

        m3 = SM[:].rearrange("p (n k) -> p n k", n=16)
        m4 = SM2[:].rearrange("p (n k) -> p n k", n=16)
        first = True
        for k in range(TOPK):
            em.vop(lambda e, k=k: e.reduce_max(S8[:, :, k:k + 1], QT[:], AX.X),
              waits=[qt_tok, iota_tok] if first else (), inc="v")
            first = False
            o("vector", lambda e, k=k: e.tensor_tensor(
                m3, QT[:], S8[:, :, k:k + 1].to_broadcast([128, 16, 64]),
                ALU.is_equal), inc="v")
            o("vector", lambda e: e.tensor_tensor(
                m4, m3, IOTA.unsqueeze(1).to_broadcast([128, 16, 64]),
                ALU.mult), inc="v")
            o("vector", lambda e: e.scalar_tensor_tensor(
                m4, m3, 64.0, m4, ALU.mult, ALU.subtract), inc="v")
            em.vop(lambda e, k=k: e.reduce_max(I8[:, :, k:k + 1], m4, AX.X),
              inc="v")
            o("vector", lambda e, k=k: e.tensor_scalar(
                I8[:, :, k:k + 1], I8[:, :, k:k + 1], 64.0, -1.0,
                ALU.subtract, ALU.mult), inc="v")
            o("vector", lambda e, k=k: e.tensor_tensor(
                m3, IOTA.unsqueeze(1).to_broadcast([128, 16, 64]),
                I8[:, :, k:k + 1].to_broadcast([128, 16, 64]), ALU.is_equal), inc="v")
            o("vector", lambda e: e.scalar_tensor_tensor(
                QT[:], m3, BIGNEG, QT[:], ALU.mult, ALU.add), inc="v")
        tk_tok = t("v")
        if DEBUG and call == 0:
            o("gpsimd", lambda e: e.dma_start(out=DBG_S8, in_=S8.ap()),
              waits=[tk_tok], inc="g", dma=True)
            o("gpsimd", lambda e: e.dma_start(out=DBG_I8, in_=I8.ap()),
              waits=[tk_tok], inc="g", dma=True)

        for p in range(2):
            o("gpsimd", lambda e, p=p: e.dma_start(
                out=bass.AP(agi.tensor, p * 16, [[32, 128], [4096, 8], [1, 8]]),
                in_=S8[:, p * 8:(p + 1) * 8, :]), waits=[tk_tok], inc="g", dma=True)
            o("gpsimd", lambda e, p=p: e.dma_start(
                out=bass.AP(agi.tensor, p * 16 + 8, [[32, 128], [4096, 8], [1, 8]]),
                in_=I8[:, p * 8:(p + 1) * 8, :]), inc="g", dma=True)
        agw = t("g")
        o("gpsimd", lambda e: e.collective_compute(
            "AllGather", ALU.bypass, replica_groups=[list(range(N_CORES))],
            ins=[agi], outs=[ago]), waits=[agw], inc="gc")
        cc_tok = t("gc")
        o("gpsimd", lambda e: e.dma_start(
            out=SX8.ap(),
            in_=ago[bass.ds(snapv("offsx", 0, N_CORES * 2 * TPC - TPC), TPC), :]
            .rearrange("(n p) k -> p n k", p=128)), waits=[cc_tok], inc="g", dma=True)
        o("gpsimd", lambda e: e.dma_start(
            out=SY8.ap(),
            in_=ago[bass.ds(snapv("offsy", 0, N_CORES * 2 * TPC - TPC), TPC), :]
            .rearrange("(n p) k -> p n k", p=128)), inc="g", dma=True)
        sxy = t("g")
        if DEBUG and call == 0:
            o("gpsimd", lambda e: e.dma_start(out=DBG_SX, in_=SX8.ap()),
              waits=[sxy], inc="g", dma=True)
            o("gpsimd", lambda e: e.dma_start(out=DBG_SY, in_=SY8.ap()),
              waits=[sxy], inc="g", dma=True)

        for Hh in range(2):
            hb = Hh * 512
            o("sync", lambda e, hb=hb: e.dma_start(
                out=SCv[:, 0:16, :],
                in_=LN_fm.rearrange("(c p) t -> p c t", p=128)[:, :, hb:hb + 512]),
                waits=[ln_all, t("t")], inc="w", dma=True)
            lnh = t("w")
            if DEBUG and call == 0 and Hh == 0:
                o("gpsimd", lambda e: e.dma_start(out=DBG_LNF, in_=SCv[:, 0:16, :]),
                  waits=[lnh], inc="g", dma=True)
                lnh = t("g")
            elif F_LNF:
                o("gpsimd", lambda e: e.nop(), waits=[lnh], inc="gc")
                lnh = t("gc")
            for eighth in range(8):
                jlist = range(eighth * 8, eighth * 8 + 8)
                h_toks = []
                prev_h = pfree.get("hprev")
                for idx, j in enumerate(jlist):
                    bg = (2 * idx) % 3
                    bu_ = (2 * idx + 1) % 3
                    o("sync", lambda e, j=j, bg=bg: e.dma_start(
                        out=WGU[bg][:, 0:16, :],
                        in_=gate_w[:, j * 128:(j + 1) * 128].rearrange(
                            "(c p) f -> p c f", p=128)),
                        waits=[gu_free[bg]], inc="w", dma=True)
                    gl = t("w")
                    o("sync", lambda e, j=j, bu_=bu_: e.dma_start(
                        out=WGU[bu_][:, 0:16, :],
                        in_=up_w[:, j * 128:(j + 1) * 128].rearrange(
                            "(c p) f -> p c f", p=128)),
                        waits=[gu_free[bu_]], inc="w", dma=True)
                    ul = t("w")
                    for c in range(NCH):
                        o("tensor", lambda e, c=c, bg=bg: e.matmul(
                            ps_a[:, 0:512], WGU[bg][:, c, :], SCv[:, c, :],
                            start=(c == 0), stop=(c == NCH - 1)),
                            waits=[gl, lnh, pfree["a"]] if c == 0 else (),
                            inc="t" if c == NCH - 1 else None)
                    gmm = t("t")
                    gu_free[bg] = gmm
                    for c in range(NCH):
                        o("tensor", lambda e, c=c, bu_=bu_: e.matmul(
                            ps_b[:, 0:512], WGU[bu_][:, c, :], SCv[:, c, :],
                            start=(c == 0), stop=(c == NCH - 1)),
                            waits=[ul, pfree["b"]] if c == 0 else (),
                            inc="t" if c == NCH - 1 else None)
                    umm = t("t")
                    gu_free[bu_] = umm
                    o("scalar", lambda e: e.activation(SIL[:], ps_a[:, 0:512], AF.Silu),
                      waits=[gmm, prev_h], inc="s")
                    sil = t("s")
                    pfree["a"] = sil
                    if F_SIL:
                        o("gpsimd", lambda e: e.nop(), waits=[sil], inc="gc")
                        sil = t("gc")
                        pfree["a"] = sil
                    if DEBUG and call == 0 and Hh == 0 and eighth == 0 and idx == 0:
                        o("gpsimd", lambda e: e.dma_start(out=DBG_SIL, in_=SIL[:]),
                          waits=[sil], inc="g", dma=True)
                        o("vector", lambda e: e.tensor_copy(SM2[:, 0:512], ps_b[:, 0:512]),
                          waits=[t("t")], inc="v")
                        _upt = t("v")
                        o("gpsimd", lambda e: e.dma_start(out=DBG_UP, in_=SM2[:, 0:512]),
                          waits=[_upt], inc="g", dma=True)
                        sil = t("g")
                    o("vector", lambda e, idx=idx: e.tensor_tensor(
                        HSL[idx], SIL[:], ps_b[:, 0:512], ALU.mult),
                        waits=[sil, umm] + ([t("g")] if (DEBUG and call == 0 and
                                                         Hh == 0 and eighth == 1)
                                            else []) + ([dn_free[0], dn_free[1]]
                                                        if F_H else []), inc="v")
                    h_toks.append(t("v"))
                    pfree["b"] = h_toks[-1]
                    prev_h = h_toks[-1]
                    pfree["hprev"] = prev_h
                h_done = h_toks[-1]
                if DEBUG and call == 0 and Hh == 0 and eighth == 0:
                    for _i in range(8):
                        o("gpsimd", lambda e, _i=_i: e.dma_start(
                            out=DBG_H[:, _i, :], in_=HSL[_i]),
                            waits=[h_done], inc="g", dma=True)
                    _hdmp = t("g")
                for i in range(NCH):
                    buf = i % 2
                    o("sync", lambda e, i=i, eighth=eighth, buf=buf: e.dma_start(
                        out=WDN[buf][:, 0:8, :],
                        in_=down_w[eighth * 1024:(eighth + 1) * 1024,
                                   i * 128:(i + 1) * 128].rearrange(
                            "(c p) f -> p c f", p=128)),
                        waits=[dn_free[buf]], inc="w", dma=True)
                    dl = t("w")
                    pn = "a" if i % 2 == 0 else "b"
                    psx = ps_a if i % 2 == 0 else ps_b
                    for idx in range(8):
                        o("tensor", lambda e, idx=idx, buf=buf, psx=psx: e.matmul(
                            psx[:, 0:512], WDN[buf][:, idx, :], HSL[idx],
                            start=(idx == 0), stop=(idx == 7)),
                            waits=[dl, h_done, pfree[pn]] if idx == 0 else (),
                            inc="t" if idx == 7 else None)
                    dmm = t("t")
                    dn_free[buf] = dmm
                    o("vector", lambda e, i=i, hb=hb, psx=psx: e.scalar_tensor_tensor(
                        XB[:, i, hb:hb + 512], psx[:, 0:512], 0.5,
                        XB[:, i, hb:hb + 512], ALU.mult, ALU.add),
                        waits=[dmm], inc="v")
                    pfree[pn] = t("v")
        mlp_tok = t("v")
        if DEBUG and call == 0:
            o("gpsimd", lambda e: e.dma_start(
                out=DBG_MLP.rearrange("(c p) t -> p c t", p=128), in_=XB.ap()),
                waits=[mlp_tok], inc="g", dma=True)

        prev_mult = None
        gb2_free = None
        tb_free = None
        prev_silw = None
        for tile in range(8):
            o("gpsimd", lambda e, tile=tile: e.dma_start(
                out=TB[:], in_=LN_tm[tile * 128:(tile + 1) * 128, :]),
                waits=[ln_all, tb_free], inc="g", dma=True)
            lntm_t = t("g")
            sx = SX8[:, tile, 0:8]; ix = SX8[:, tile, 8:16]
            sy = SY8[:, tile, 0:8]; iy = SY8[:, tile, 8:16]
            o("vector", lambda e, ix=ix: e.tensor_scalar_mul(SM2[:, 32:40], ix, 64.0),
              waits=[sxy], inc="v")
            for a in range(8):
                o("vector", lambda e, a=a, sy=sy, sx=sx: e.tensor_single_scalar(
                    CMB[:, a * 8:(a + 1) * 8], sy, sx[:, a:a + 1], ALU.add), inc="v")
                o("vector", lambda e, a=a, iy=iy: e.tensor_single_scalar(
                    CMI[:, a * 8:(a + 1) * 8], iy, SM2[:, 32 + a:33 + a], ALU.add),
                    inc="v")
            built = t("v")
            cm3 = CMB[:].unsqueeze(1)
            for k in range(TOPK):
                em.vop(lambda e, k=k: e.reduce_max(
                    SM2[:, 0:1].unsqueeze(1), cm3, AX.X), inc="v")
                o("vector", lambda e, k=k: e.tensor_copy(EW[:, k:k + 1], SM2[:, 0:1]),
                  inc="v")
                o("vector", lambda e: e.tensor_single_scalar(
                    SM[:, 0:64], CMB[:], SM2[:, 0:1], ALU.is_equal), inc="v")
                o("vector", lambda e: e.tensor_mul(SM[:, 64:128], SM[:, 0:64], IOTA),
                  inc="v")
                o("vector", lambda e: e.scalar_tensor_tensor(
                    SM[:, 64:128], SM[:, 0:64], 64.0, SM[:, 64:128],
                    ALU.mult, ALU.subtract), inc="v")
                em.vop(lambda e: e.reduce_max(
                    SM2[:, 1:2].unsqueeze(1),
                    SM[:, 64:128].unsqueeze(1), AX.X), inc="v")
                o("vector", lambda e: e.tensor_scalar(
                    SM2[:, 1:2], SM2[:, 1:2], 64.0, -1.0, ALU.subtract, ALU.mult),
                    inc="v")
                o("vector", lambda e: e.tensor_single_scalar(
                    SM[:, 0:64], IOTA, SM2[:, 1:2], ALU.is_equal), inc="v")
                o("vector", lambda e: e.tensor_mul(SM[:, 64:128], SM[:, 0:64], CMI[:]),
                  inc="v")
                em.vop(lambda e, k=k: e.reduce_sum(
                    SM2[:, 2 + k:3 + k].unsqueeze(1),
                    SM[:, 64:128].unsqueeze(1), AX.X), inc="v")
                o("vector", lambda e: e.scalar_tensor_tensor(
                    CMB[:], SM[:, 0:64], BIGNEG, CMB[:], ALU.mult, ALU.add), inc="v")
            sel_tok = t("v")
            o("vector", lambda e: e.tensor_copy(EIDI[:], SM2[:, 2:10]),
              waits=[t("g")], inc="v")
            eid_tok = t("v")
            em.vop(lambda e: e.reduce_max(
                SM2[:, 10:11].unsqueeze(1),
                EW[:].unsqueeze(1), AX.X), inc="v")
            o("vector", lambda e: e.tensor_scalar_mul(SM2[:, 10:11], SM2[:, 10:11],
                                                      -1.0), inc="v")
            nm = t("v")
            o("scalar", lambda e: e.activation(EW[:], EW[:], AF.Exp, bias=SM2[:, 10:11],
                                               accum_out=SM2[:, 11:12]),
              waits=[nm], inc="s")
            ex = t("s")
            em.vop(lambda e: e.reciprocal(SM2[:, 12:13], SM2[:, 11:12]),
              waits=[ex], inc="v")
            o("vector", lambda e: e.tensor_single_scalar(EW[:], EW[:], SM2[:, 12:13],
                                                         ALU.mult), inc="v")
            smw = t("v")

            for k in range(TOPK):
                o("gpsimd", lambda e, k=k: e.indirect_dma_start(
                    out=GB1, out_offset=None, in_=down_embed,
                    in_offset=IndirectOffsetOnAxis(EIDI[:, k:k + 1], 0)),
                    waits=[eid_tok, prev_mult], inc="g", dma=True)
                dg = t("g")
                o("vector", lambda e: e.tensor_mul(GB2, GB1, TB[:]),
                  waits=[dg, lntm_t, gb2_free], inc="v")
                prev_mult = t("v")
                em.vop(lambda e, k=k: e.reduce_sum(
                    SM2[:, 16 + k:17 + k].unsqueeze(1),
                    GB2.unsqueeze(1), AX.X),
                    waits=[prev_silw] if k == 0 else (), inc="v")
                gb2_free = t("v")
            dots = t("v")
            o("scalar", lambda e: e.activation(SM2[:, 24:32], SM2[:, 16:24], AF.Silu),
              waits=[dots], inc="s")
            silw = t("s")
            prev_silw = silw
            o("vector", lambda e: e.tensor_mul(SM2[:, 24:32], SM2[:, 24:32], EW[:]),
              waits=[silw, smw], inc="v")
            wfin = t("v")

            for k in range(TOPK):
                o("gpsimd", lambda e, k=k: e.indirect_dma_start(
                    out=GB1, out_offset=None, in_=up_embed,
                    in_offset=IndirectOffsetOnAxis(EIDI[:, k:k + 1], 0)),
                    waits=[prev_mult], inc="g", dma=True)
                ug = t("g")
                if k == 0:
                    o("vector", lambda e, k=k: e.tensor_single_scalar(
                        GB2, GB1, SM2[:, 24 + k:25 + k], ALU.mult),
                        waits=[ug, wfin, gb2_free], inc="v")
                else:
                    o("vector", lambda e, k=k: e.scalar_tensor_tensor(
                        GB2, GB1, SM2[:, 24 + k:25 + k], GB2, ALU.mult, ALU.add),
                        waits=[ug], inc="v")
                prev_mult = t("v")
            acc = t("v")
            for c in range(NCH):
                o("tensor", lambda e, c=c: e.transpose(
                    ps_big[:, c * 128:(c + 1) * 128], GB2[:, c * 128:(c + 1) * 128],
                    IDT), waits=[acc, pfree["big"]] if c == 0 else (),
                    inc="t" if c == NCH - 1 else None)
            tre = t("t")
            gb2_free = tre
            tb_free = dots
            for c in range(NCH):
                o("vector", lambda e, c=c, tile=tile: e.scalar_tensor_tensor(
                    XB[:, c, tile * 128:tile * 128 + 128],
                    ps_big[:, c * 128:(c + 1) * 128], 0.5,
                    XB[:, c, tile * 128:tile * 128 + 128], ALU.mult, ALU.add),
                    waits=[tre] if c == 0 else (), inc="v" if c == NCH - 1 else None)
            pfree["big"] = t("v")
        return t("v")

    def attention(XB, OUTB, xtok):
        LN2 = OUTB
        lntm, fm = None, None
        for tile in range(8):
            lntm, fm = ln_tile(XB, tile,
                               LN2.ap()[:, :, tile * 128:tile * 128 + 128], xtok,
                               ta_waits=[lntm] if lntm else (),
                               tb_waits=[fm] if fm else ())
        ln_tok = fm

        if DEBUG:
            o("gpsimd", lambda e: e.dma_start(
                out=DBG_LN2.rearrange("(c p) t -> p c t", p=128), in_=LN2.ap()),
                waits=[ln_tok], inc="g", dma=True)
            o("gpsimd", lambda e: e.dma_start(out=DBG_STAT, in_=STAT[:, :]),
                waits=[ln_tok], inc="g", dma=True)
        WSL = [WB.ap()[:, 0:2, :].rearrange("p a (c f) -> p (a c) f", f=128),
               WB.ap()[:, 2:4, :].rearrange("p a (c f) -> p (a c) f", f=128)]
        w_free = [xtok, xtok]
        stage_free = None

        def proj_loop(Wt, scale_ap, bias_ap, dst_fn):
            nonlocal stage_free
            for h in range(H_):
                buf = h % 2
                o("sync", lambda e, h=h, buf=buf, Wt=Wt: e.dma_start(
                    out=WSL[buf][:, 0:16, :],
                    in_=Wt[:, h * 128:(h + 1) * 128].rearrange("(c p) f -> p c f", p=128)),
                    waits=[w_free[buf]], inc="w", dma=True)
                wl = t("w")
                for half in range(2):
                    pn = "a" if half == 0 else "b"
                    psx = ps_a if half == 0 else ps_b
                    for c in range(NCH):
                        o("tensor", lambda e, c=c, buf=buf, hf=half, psx=psx: e.matmul(
                            psx[:, 0:512], WSL[buf][:, c, :],
                            LN2[:, c, hf * 512:(hf + 1) * 512],
                            start=(c == 0), stop=(c == NCH - 1)),
                            waits=[wl, ln_tok, pfree[pn]] if c == 0 else (),
                            inc="t" if c == NCH - 1 else None)
                    mm = t("t")
                    if half == 1:
                        w_free[buf] = mm
                    o("vector", lambda e, h=h, hf=half, psx=psx, sa=scale_ap,
                      ba=bias_ap: e.tensor_scalar(
                        TA[:, hf * 512:(hf + 1) * 512], psx[:, 0:512],
                        sa[:, h:h + 1], ba[:, h:h + 1], ALU.mult, ALU.add),
                        waits=[mm, scales_tok, stage_free], inc="v")
                    pfree[pn] = t("v")
                ev = t("v")
                dst_fn(h, ev)
                stage_free = t("g")

        def k_dst(h, ev):
            if DEBUG and h == 0:
                o("gpsimd", lambda e: e.dma_start(out=DBG_TA0, in_=TA[:]),
                  waits=[ev], inc="g", dma=True)
            o("gpsimd", lambda e, h=h: e.dma_start(
                out=K_loc[h * 128:(h + 1) * 128, :],
                in_=TA[:, 0:1024]), waits=[ev], inc="g", dma=True)
        proj_loop(WkT, KSC, KBE, k_dst)
        kw_loc = t("g")
        o("gpsimd", lambda e: e.dma_start(
            out=K_sh[bass.ds(snapv("slot", 0, 1) * D_, D_), :],
            in_=K_loc.bitcast(U32)), waits=[kw_loc], inc="g", dma=True)
        proj_loop(WqT, QSC, QBE, lambda h, ev: o(
            "gpsimd", lambda e, h=h: e.dma_start(
                out=Q_loc[h * 128:(h + 1) * 128, :], in_=TA[:, 0:1024]),
            waits=[ev], inc="g", dma=True))

        WV = WB.ap()[:, 0:4, :].rearrange("p a (c f) -> p (a c) f", f=256)
        vst_free = None
        last_v_mm = None
        for vs in range(8):
            o("sync", lambda e, vs=vs: e.dma_start(
                out=WV[:, 0:16, :],
                in_=WvT[:, vs * 256:(vs + 1) * 256].rearrange("(c p) f -> p c f", p=128)),
                waits=[w_free[0], w_free[1]] if vs == 0 else [last_v_mm],
                inc="w", dma=True)
            wvl = t("w")
            for tt_ in range(8):
                pn = "a" if tt_ % 2 == 0 else "b"
                psx = ps_a if tt_ % 2 == 0 else ps_b
                for c in range(NCH):
                    o("tensor", lambda e, c=c, tt_=tt_, psx=psx: e.matmul(
                        psx[:, 0:256], LN2[:, c, tt_ * 128:(tt_ + 1) * 128],
                        WV[:, c, :], start=(c == 0), stop=(c == NCH - 1)),
                        waits=[wvl, pfree[pn]] if c == 0 else (),
                        inc="t" if c == NCH - 1 else None)
                vmm = t("t")
                last_v_mm = vmm
                o("vector", lambda e, vs=vs, psx=psx: e.tensor_add(
                    SM3[:, 0:256], psx[:, 0:256], BVB[:, vs * 256:(vs + 1) * 256]),
                    waits=[vmm, bvb_tok, vst_free], inc="v")
                vadd = t("v")
                pfree[pn] = vadd
                o("gpsimd", lambda e, vs=vs, tt_=tt_: e.dma_start(
                    out=V_loc[tt_ * 128:(tt_ + 1) * 128, vs * 256:(vs + 1) * 256],
                    in_=SM3[:, 0:256]),
                    waits=[vadd], inc="g", dma=True)
                vst_free = t("g")
        vw_loc = t("g")
        o("gpsimd", lambda e: e.dma_start(
            out=V_sh[bass.ds(snapv("slot", 0, 1) * TPC, TPC), :],
            in_=V_loc.bitcast(U32)), waits=[vw_loc], inc="g", dma=True)
        kvw = t("g")

        o("gpsimd", lambda e: e.dma_start(
            out=X_stash.rearrange("(c p) t -> p c t", p=128), in_=XB.ap()),
            waits=[xtok], inc="g", dma=True)
        xst = t("g")
        o("gpsimd", lambda e: e.dma_start(out=bar_in, in_=BARZ),
            waits=[scales_tok], inc="g", dma=True)
        o("gpsimd", lambda e: e.collective_compute(
            "AllReduce", ALU.add, replica_groups=[[0, 1], [2, 3], [4, 5], [6, 7]],
            ins=[bar_in], outs=[bar_out]), waits=[kvw], inc="gc")
        kvready = t("gc")

        if DEBUG:
            o("gpsimd", lambda e: e.dma_start(out=DBG_KSH, in_=K_sh.bitcast(F32)),
              waits=[kvready], inc="g", dma=True)
            o("gpsimd", lambda e: e.dma_start(out=DBG_KLOC, in_=K_loc),
              waits=[kvready], inc="g", dma=True)
        if STOP_KV:
            return t("g")
        OBUF = XB
        PT = OUTB
        KH = [TB.ap(), WB.ap()[:, 0:2, :].rearrange("p a f -> p (a f)")]
        QH = [WB.ap()[:, 2, :], WB.ap()[:, 3, :]]
        VH = WB.ap()[:, 4:6, :].rearrange("p a (c f) -> p (a c) f", f=128)

        def load_kq(h, extra=()):
            buf = h % 2
            for s in range(2):
                o("gpsimd", lambda e, h=h, s=s, buf=buf: e.dma_start(
                    out=KH[buf][:, s * 1024:(s + 1) * 1024],
                    in_=K_sh[s * D_ + h * 128:s * D_ + (h + 1) * 128, :].bitcast(F32)),
                    waits=(list(extra) + [kvready]) if s == 0 else (),
                    inc="g", dma=True)
            o("gpsimd", lambda e, h=h, buf=buf: e.dma_start(
                out=QH[buf], in_=Q_loc[h * 128:(h + 1) * 128, :]), inc="g", dma=True)
            return t("g")

        def load_v(h, extra=()):
            for s in range(2):
                o("gpsimd", lambda e, h=h, s=s: e.dma_start(
                    out=VH[:, s * 8:(s + 1) * 8, :],
                    in_=V_sh[s * TPC:(s + 1) * TPC,
                             h * 128:(h + 1) * 128].bitcast(F32).rearrange(
                        "(c p) f -> p c f", p=128)),
                    waits=(list(extra) + [kvready]) if s == 0 else (),
                    inc="g", dma=True)
            return t("g")

        kq_tok = load_kq(0, extra=[stage_free, xst])
        prev_ptcopy = None
        prev_av = None
        prev_pexp = None
        for h in range(H_):
            buf = h % 2
            for qt_ in range(8):
                for tc in range(4):
                    o("tensor", lambda e, qt_=qt_, tc=tc, buf=buf: e.matmul(
                        ps_big[:, tc * 512:(tc + 1) * 512],
                        QH[buf][:, qt_ * 128:(qt_ + 1) * 128],
                        KH[buf][:, tc * 512:(tc + 1) * 512],
                        start=True, stop=True),
                        waits=[kq_tok, pfree["big"], ln_tok] if (qt_ == 0 and tc == 0)
                        else (), inc="t" if tc == 3 else None)
                smm = t("t")
                em.vop(lambda e: e.reduce_max(
                    SM2[:, 0:1].unsqueeze(1),
                    ps_big[:].unsqueeze(1), AX.X), waits=[smm, prev_pexp], inc="v")
                o("vector", lambda e: e.tensor_scalar_mul(SM2[:, 0:1], SM2[:, 0:1],
                                                          -1.0), inc="v")
                nmax = t("v")
                o("scalar", lambda e: e.activation(TA[:], ps_big[:], AF.Exp,
                                                   bias=SM2[:, 0:1],
                                                   accum_out=SM2[:, 1:2]),
                  waits=[nmax, prev_ptcopy], inc="s")
                pexp = t("s")
                prev_pexp = pexp
                pfree["big"] = pexp
                em.vop(lambda e: e.reciprocal(SM2[:, 2:3], SM2[:, 1:2]),
                  waits=[pexp], inc="v")
                o("vector", lambda e: e.tensor_single_scalar(TA[:], TA[:], SM2[:, 2:3],
                                                             ALU.mult), inc="v")
                pnorm = t("v")
                for g2 in range(2):
                    for tc in range(8):
                        o("tensor", lambda e, tc=tc, g2=g2: e.transpose(
                            ps_c[:, tc * 128:(tc + 1) * 128],
                            TA[:, (g2 * 8 + tc) * 128:(g2 * 8 + tc + 1) * 128], IDT),
                            waits=[pnorm, pfree["c"]] if tc == 0 else (),
                            inc="t" if tc == 7 else None)
                    trp = t("t")
                    o("vector", lambda e, qt_=qt_, g2=g2: e.tensor_copy(
                        PT.ap()[:, g2 * 8:(g2 + 1) * 8, qt_ * 128:(qt_ + 1) * 128],
                        ps_c[:].rearrange("p (c f) -> p c f", c=8)),
                        waits=[trp, prev_av], inc="v")
                    prev_ptcopy = t("v")
                    pfree["c"] = prev_ptcopy
            pt_tok = prev_ptcopy
            v_tok = load_v(h, extra=[prev_av])
            kq_next = load_kq(h + 1, extra=[t("t")]) if h + 1 < H_ else None
            for qs in range(2):
                pn = "a" if qs == 0 else "b"
                psx = ps_a if qs == 0 else ps_b
                for tc in range(16):
                    o("tensor", lambda e, qs=qs, tc=tc, psx=psx: e.matmul(
                        psx[:, 0:512], VH[:, tc, :],
                        PT.ap()[:, tc, qs * 512:(qs + 1) * 512],
                        start=(tc == 0), stop=(tc == 15)),
                        waits=[pt_tok, v_tok, pfree[pn]] if tc == 0 else (),
                        inc="t" if tc == 15 else None)
                avm = t("t")
                o("vector", lambda e, h=h, qs=qs, psx=psx: e.tensor_copy(
                    OBUF[:, h, qs * 512:(qs + 1) * 512], psx[:, 0:512]),
                    waits=[avm, xst], inc="v")
                pfree[pn] = t("v")
            prev_av = t("v")
            if kq_next is not None:
                kq_tok = kq_next
        o_tok = prev_av
        if F_O:
            o("gpsimd", lambda e: e.dma_start(
                out=LN_fm.rearrange("(c p) t -> p c t", p=128), in_=OBUF.ap()),
                waits=[o_tok], inc="g", dma=True)
            o_tok = t("g")
        if DEBUG:
            o("gpsimd", lambda e: e.dma_start(
                out=DBG_O.rearrange("(c p) t -> p c t", p=128), in_=OBUF.ap()),
                waits=[o_tok], inc="g", dma=True)
            o_tok = t("g")

        WOB = [WB.ap()[:, 0:2, :].rearrange("p a (c f) -> p (a c) f", f=128),
               WB.ap()[:, 2:4, :].rearrange("p a (c f) -> p (a c) f", f=128)]
        wo_free = [o_tok, o_tok]
        xres_free = None
        for i in range(NCH):
            buf = i % 2
            o("sync", lambda e, i=i, buf=buf: e.dma_start(
                out=WOB[buf][:, 0:16, :],
                in_=WoT[:, i * 128:(i + 1) * 128].rearrange("(c p) f -> p c f", p=128)),
                waits=[wo_free[buf]], inc="w", dma=True)
            wl = t("w")
            o("gpsimd", lambda e, i=i: e.dma_start(
                out=TA[:, 0:1024], in_=X_stash[i * 128:(i + 1) * 128, :]),
                waits=[xst, xres_free, o_tok], inc="g", dma=True)
            xld = t("g")
            for half in range(2):
                pn = "a" if half == 0 else "b"
                psx = ps_a if half == 0 else ps_b
                for c in range(NCH):
                    o("tensor", lambda e, c=c, buf=buf, hf=half, psx=psx: e.matmul(
                        psx[:, 0:512], WOB[buf][:, c, :],
                        OBUF[:, c, hf * 512:(hf + 1) * 512],
                        start=(c == 0), stop=(c == NCH - 1)),
                        waits=[wl, o_tok, pfree[pn]] if c == 0 else (),
                        inc="t" if c == NCH - 1 else None)
                omm = t("t")
                if half == 1:
                    wo_free[buf] = omm
                o("vector", lambda e, i=i, hf=half, psx=psx: e.tensor_scalar_add(
                    TB[:, hf * 512:(hf + 1) * 512], psx[:, 0:512],
                    BOc[:, i:i + 1]), waits=[omm], inc="v")
                ob = t("v")
                pfree[pn] = ob
                o("vector", lambda e, i=i, hf=half: e.tensor_add(
                    OUTB[:, i, hf * 512:(hf + 1) * 512],
                    TB[:, hf * 512:(hf + 1) * 512], TA[:, hf * 512:(hf + 1) * 512]),
                    waits=[ob, xld], inc="v")
            xres_free = t("v")
        return t("v")

    tok0 = ffn(0, XT0, AR, ldx)
    if DEBUG:
        o("gpsimd", lambda e: e.dma_start(
            out=DBG_X1.rearrange("(c p) t -> p c t", p=128), in_=XT0.ap()),
            waits=[tok0], inc="g", dma=True)
    tok1 = attention(XT0, AR, tok0)
    if STOP_KV:
        o("gpsimd", lambda e: e.dma_start(
            out=yT.rearrange("(c p) t -> p c t", p=128), in_=XT0.ap()),
            waits=[tok1], inc="g", dma=True)
        o("gpsimd", lambda e: e.nop(), waits=[t("g")])
        em.emit()
        return nc
    if DEBUG:
        o("gpsimd", lambda e: e.dma_start(
            out=DBG_X2.rearrange("(c p) t -> p c t", p=128), in_=AR.ap()),
            waits=[tok1], inc="g", dma=True)
    tok2 = ffn(1, AR, XT0, tok1)

    o("gpsimd", lambda e: e.dma_start(out=yT.rearrange("(c p) t -> p c t", p=128),
                                      in_=AR.ap()), waits=[tok2], inc="g", dma=True)
    fin = t("g")
    o("gpsimd", lambda e: e.nop(), waits=[fin])

    em.emit()
    return nc


_NC_CACHE = {}


def kernel(L0_attn_x, L0_attn_Wq, L0_attn_bq, L0_attn_Wk, L0_attn_bk,
           L0_attn_Wv, L0_attn_bv, L0_attn_Wo, L0_attn_bo,
           L0_attn_spiral_weight, L0_attn_helix_rotation,
           L0_ffn_W_q, L0_ffn_keys, L0_ffn_down_embed, L0_ffn_up_embed,
           L0_ffn_gate_w, L0_ffn_up_w, L0_ffn_down_w):
    x = np.ascontiguousarray(np.asarray(L0_attn_x, dtype=np.float32).reshape(NTOK, D_))
    WqT = np.ascontiguousarray(np.asarray(L0_attn_Wq, np.float32).T)
    WkT = np.ascontiguousarray(np.asarray(L0_attn_Wk, np.float32).T)
    WvT = np.ascontiguousarray(np.asarray(L0_attn_Wv, np.float32).T)
    WoT = np.ascontiguousarray(np.asarray(L0_attn_Wo, np.float32).T)
    keys = np.asarray(L0_ffn_keys, np.float32)

    if "nc" not in _NC_CACHE:
        _NC_CACHE["nc"] = build_megakernel()
    nc = _NC_CACHE["nc"]

    in_maps = []
    for c in range(N_CORES):
        in_maps.append({
            "xT": np.ascontiguousarray(x[c * TPC:(c + 1) * TPC].T),
            "WqT": WqT, "WkT": WkT, "WvT": WvT, "WoT": WoT,
            "bq": np.asarray(L0_attn_bq, np.float32),
            "bk": np.asarray(L0_attn_bk, np.float32),
            "bv": np.asarray(L0_attn_bv, np.float32),
            "bo": np.asarray(L0_attn_bo, np.float32),
            "spiral": np.asarray(L0_attn_spiral_weight, np.float32),
            "helix": np.asarray(L0_attn_helix_rotation, np.float32),
            "W_q": np.asarray(L0_ffn_W_q, np.float32),
            "keys_mine": np.ascontiguousarray(keys[0 if c < 4 else 1]),
            "down_embed": np.asarray(L0_ffn_down_embed, np.float32),
            "up_embed": np.asarray(L0_ffn_up_embed, np.float32),
            "gate_w": np.asarray(L0_ffn_gate_w, np.float32),
            "up_w": np.asarray(L0_ffn_up_w, np.float32),
            "down_w": np.asarray(L0_ffn_down_w, np.float32),
            "agoff": np.array([[(c // 2) * 2 * TPC + (c % 2) * TPC,
                                (4 + c // 2) * 2 * TPC + (c % 2) * TPC]],
                              dtype=np.int32),
            "slot": np.array([[c % 2]], dtype=np.int32),
        })

    res = run_bass_kernel_spmd(nc, in_maps, core_ids=list(range(N_CORES)))
    out = np.empty((NTOK, D_), dtype=np.float32)
    for c in range(N_CORES):
        out[c * TPC:(c + 1) * TPC] = res.results[c]["yT"].T
    return out.reshape(B_, S_, D_)
